# version 17
# baseline (speedup 1.0000x reference)
import numpy as np
import ml_dtypes
from contextlib import ExitStack

import concourse.bass as bass
import concourse.mybir as mybir
from concourse.bass_utils import run_bass_kernel_spmd

F32 = mybir.dt.float32
BF16 = mybir.dt.bfloat16
FP8 = mybir.dt.float8e4
AF = mybir.ActivationFunctionType
ALU = mybir.AluOpType
AX = mybir.AxisListType

CUSH = 70.0
EPS = 1e-5
NEG = -1e30

FULL_CFG = dict(H=2048, NH=16, TQ=1024, S=4096, FFI=8192)


def derive(cfg):
    c = dict(cfg)
    c["HC"] = c["H"] // 128
    c["SCALE"] = float(1.0 / np.sqrt(128.0))
    c["QGW"] = min(512, c["TQ"])
    c["NQG"] = c["TQ"] // c["QGW"]
    c["QSUBG"] = c["QGW"] // 128
    c["MEGA"] = min(512, c["S"])
    c["NMEGA"] = c["S"] // c["MEGA"]
    c["KSUB"] = c["MEGA"] // 128
    c["NKSLOT"] = c["S"] // 128
    c["NQS"] = c["TQ"] // 128
    c["OWN0"] = c["NKSLOT"] - c["NQS"]
    c["VFW"] = min(512, c["H"])
    c["VFG"] = c["H"] // c["VFW"]
    c["FFB"] = c["FFI"] // 128
    c["NCH"] = max(2, c["FFI"] // 2048)
    c["FFHALF"] = c["FFI"] // c["NCH"]
    c["DQ"] = min(2048, c["FFHALF"])
    c["NDQ"] = c["FFHALF"] // c["DQ"]
    return c


class Rec:
    ENGS = ("sync", "gpsimd", "tensor", "vector", "scalar")

    def __init__(self, nc):
        self.nc = nc
        self.stages = [[]]
        self.CN = {}
        self.es = ExitStack()

    def sem(self, name):
        s = self.es.enter_context(self.nc.semaphore(name))
        self.CN[s] = 0
        return s

    def barrier(self):
        if self.stages[-1]:
            self.stages.append([])

    def op(self, eng, fn, waits=(), incs=()):
        rw = tuple((s, int(v)) for (s, v) in waits)
        ri = []
        out = None
        for (s, k) in incs:
            self.CN[s] += k
            ri.append((s, k))
            if out is None:
                out = self.CN[s]
        self.stages[-1].append((eng, fn, rw, tuple(ri)))
        return out

    def build(self):
        for stage in self.stages:
            if not stage:
                continue
            with self.nc.Block() as blk:
                for engname in self.ENGS:
                    ops = [o for o in stage if o[0] == engname]
                    if not ops:
                        continue
                    def body(eng, ops=ops):
                        for (_, fn, waits, incs) in ops:
                            for (s, v) in waits:
                                eng.wait_ge(s, v)
                            inst = fn(eng)
                            for (s, k) in incs:
                                inst.then_inc(s, k)
                    getattr(blk, engname)(body)


def build_nc(cfg):
    c = derive(cfg)
    H, NH, TQ, S, FFI = c["H"], c["NH"], c["TQ"], c["S"], c["FFI"]
    HC, SCALE = c["HC"], c["SCALE"]
    QGW, NQG, QSUBG = c["QGW"], c["NQG"], c["QSUBG"]
    MEGA, NMEGA, KSUB = c["MEGA"], c["NMEGA"], c["KSUB"]
    NKSLOT, OWN0 = c["NKSLOT"], c["OWN0"]
    VFW, VFG = c["VFW"], c["VFG"]
    FFB, FFHALF, DQ, NDQ = c["FFB"], c["FFHALF"], c["DQ"], c["NDQ"]
    NCH = c["NCH"]
    HINV = float(1.0 / H)
    NHB = FFHALF // 128
    SEGW = max(TQ, MEGA)

    PHASES = int(cfg.get("phases", 3))
    FFNLITE = int(cfg.get("ffnlite", 3))
    nc = bass.Bass("TRN2", target_bir_lowering=False)
    R = Rec(nc)

    def din(name, shape, dt=F32):
        return nc.dram_tensor(name, list(shape), dt, kind="ExternalInput").ap()

    xs = din("xs", (H, S))
    wq_cs = din("wq_cs", (NH, 128, H))
    wk_cs = din("wk_cs", (NH, 128, H))
    wv_rs = din("wv_rs", (VFG, 128, HC * VFW))
    wo_cs = din("wo_cs", (HC, 128, H))
    wg_rows = din("wg_rows", (FFB, 128, H))
    wu_rows = din("wu_rows", (FFB, 128, H))
    wd_rows = din("wd_rows", (HC, 128, FFI))
    bqkv_t = din("bqkv_t", (128, 3 * HC))
    bout_t = din("bout_t", (128, HC))
    bv_bc = din("bv_bc", (128, H), BF16)
    tri_c = din("tri_in", (128, 128))
    flag_c = din("flag_in", (128, NKSLOT))
    ind_b = din("ind_b", (128, NH * NH), BF16)
    id_in = din("id_in", (128, 128))

    dacc_hbm = nc.dram_tensor("dacc_hbm", [H, TQ], F32, kind="Internal").ap()
    resid_hbm = nc.dram_tensor("resid_hbm", [H, TQ], F32, kind="Internal").ap()
    outT = nc.dram_tensor("outT", [H, TQ], F32, kind="ExternalOutput").ap()
    DBG = bool(cfg.get("dbg"))
    if DBG:
        dbgK = nc.dram_tensor("dbgK", [NMEGA * 128, NH * MEGA], BF16, kind="ExternalOutput").ap()
        dbgL = nc.dram_tensor("dbgL", [NMEGA * 16, NQG * QGW], F32, kind="ExternalOutput").ap()
        dbgPT = nc.dram_tensor("dbgPT", [NMEGA * 128, NH * KSUB * QGW], BF16, kind="ExternalOutput").ap()

    es = R.es
    sbuf = lambda n, shp, dt: es.enter_context(nc.sbuf_tensor(n, list(shp), dt))
    mkps = lambda n: es.enter_context(nc.psum_tensor(n, [128, 512], F32))

    s_dm = R.sem("s_dm")
    s_dmg = R.sem("s_dmg")
    s_res0 = R.sem("s_res0")
    s_res1 = R.sem("s_res1")
    s_res = [s_res0, s_res1]
    s_out0 = R.sem("s_out0")
    s_out1 = R.sem("s_out1")
    s_out = [s_out0, s_out1]
    s_mm = R.sem("s_mm")
    s_v = R.sem("s_v")
    s_sc = R.sem("s_sc")
    s_rl = R.sem("s_rl")
    s_g = R.sem("s_g")

    CN = R.CN

    class Dstream:
        def __init__(self, name, queue, npar=2):
            self.sems = [R.sem(f"{name}{i}") for i in range(npar)]
            self.queue = queue
            self.n = 0
        def load(self, fn, waits=()):
            s = self.sems[self.n % len(self.sems)]
            self.n += 1
            cnt = R.op(self.queue, fn, waits=waits, incs=((s, 16),))
            return (s, cnt)

    stx = Dstream("stx", "sync", npar=4)
    strow = Dstream("strow", "sync")
    stm = Dstream("stm", "sync")
    swc = Dstream("swc", "gpsimd")
    swv = Dstream("swv", "gpsimd")

    def V(fn, waits=()):
        return R.op("vector", fn, waits=tuple(waits) + ((s_v, CN[s_v]),), incs=((s_v, 1),))

    def SC(fn, waits=()):
        return R.op("scalar", fn, waits=tuple(waits) + ((s_sc, CN[s_sc]),), incs=((s_sc, 1),))

    def G(fn, waits=()):
        return R.op("gpsimd", fn, waits=tuple(waits) + ((s_g, CN[s_g]),), incs=((s_g, 1),))

    def T(fn, waits=(), mark=False):
        return R.op("tensor", fn, waits=waits, incs=((s_mm, 1),) if mark else ())

    Q_sb = sbuf("Q", [128, NH * TQ], BF16)
    bq_sb = sbuf("bq", [128, 3 * HC], F32)
    bo_sb = sbuf("bo", [128, HC], F32)
    tri_sb = sbuf("tri", [128, 128], F32)
    flag_sb = sbuf("flag", [128, NKSLOT], F32)
    indb_sb = sbuf("indb", [128, NH * NH], BF16)
    ones1_sb = sbuf("ones1", [1, 128], F32)
    onesc_sb = sbuf("onesc", [128, 1], F32)
    stat_sb = sbuf("stat", [32, 64], F32)
    rowt_sb = sbuf("rowt", [32, 32], F32)
    biasrow_sb = sbuf("biasrow", [1, 4 * 32], F32)
    bias_sb = sbuf("biasx", [128, KSUB * NH], F32)
    facbc_sb = sbuf("facbc", [128, NH], F32)
    rl0_sb = sbuf("rl0", [1, 2 * 512], F32)
    rL_sb = sbuf("rL", [16, NQG * QGW], F32)
    Lacc_sb = sbuf("Lacc", [16, NQG * QGW], F32)
    id128_sb = sbuf("id128", [128, 128], F32)
    tqT_sb = sbuf("tqT", [128, 512], F32)
    tq8_sb = sbuf("tq8", [128, 512], FP8)
    tqb_sb = sbuf("tqb", [128, 512], BF16)
    bmt_sb = sbuf("bmt", [128, 16], F32)
    rsbt_sb = sbuf("rsbt", [128, 16], F32)
    s2bt_sb = sbuf("s2bt", [128, 16], F32)

    C_QMAX, C_KMAX, C_SB, C_SBN, C_FAC, C_T0, C_T1, C_T2 = range(8)
    stc = lambda col: stat_sb[0:NH, col:col + 1]

    PS_S0 = mkps("ps_s0"); PS_S1 = mkps("ps_s1")
    PS_O0 = mkps("ps_o0"); PS_O1 = mkps("ps_o1")
    PS_L0 = mkps("ps_l0"); PS_L1 = mkps("ps_l1")
    PS_X0 = mkps("ps_x0"); PS_X1 = mkps("ps_x1")
    SBANK = [PS_S0, PS_S1]
    LBANK = [PS_L0, PS_L1]
    XBANK2 = [PS_X0, PS_X1]

    for apv, dst in ((bqkv_t, bq_sb), (bout_t, bo_sb), (tri_c, tri_sb),
                     (flag_c, flag_sb), (id_in, id128_sb)):
        R.op("sync", lambda e, apv=apv, dst=dst: e.dma_start(dst[:, :], apv),
             incs=((s_dm, 16),))
    R.op("gpsimd", lambda e: e.dma_start(indb_sb[:, :], ind_b), incs=((s_dmg, 16),))
    V(lambda e: e.memset(ones1_sb[:, :], 1.0))
    V(lambda e: e.memset(onesc_sb[:, :], 1.0))
    V(lambda e: e.memset(stat_sb[:, :], 0.0))
    V(lambda e: e.memset(rowt_sb[:, :], 0.0))
    V(lambda e: e.memset(biasrow_sb[:, :], 0.0))
    cdm = CN[s_dm]
    cdw = CN[s_dmg]
    R.barrier()
    T(lambda e: e.nop(), waits=((s_dm, cdm), (s_dmg, cdw)))
    V(lambda e: e.nop(), waits=((s_dm, cdm), (s_dmg, cdw)))
    SC(lambda e: e.nop(), waits=((s_dm, cdm), (s_dmg, cdw)))

    def emit_quantT(src_f32_ap, dst_bf16_ap, width=512):
        nj = width // 32
        V(lambda e: e.transpose(tqT_sb[:, 0:width], src_f32_ap))
        V(lambda e: e.tensor_reduce(
            bmt_sb[:, 0:nj], tqT_sb[:, 0:width].rearrange("p (j b) -> p j b", j=nj),
            AX.X, ALU.max, apply_absolute_value=True))
        V(lambda e: e.tensor_scalar(s2bt_sb[:, 0:nj], bmt_sb[:, 0:nj],
                                    1.0 / 448.0, 1e-30, ALU.mult, ALU.max))
        V(lambda e: e.reciprocal(rsbt_sb[:, 0:nj], s2bt_sb[:, 0:nj]))
        V(lambda e: e.tensor_scalar_mul(rsbt_sb[:, 0:nj], rsbt_sb[:, 0:nj], 0.5))
        V(lambda e: e.tensor_scalar_mul(s2bt_sb[:, 0:nj], s2bt_sb[:, 0:nj], 2.0))
        V(lambda e: e.tensor_tensor(
            tq8_sb[:, 0:width], tqT_sb[:, 0:width],
            rsbt_sb[:, 0:nj].broadcast_to([128, nj, 32]), ALU.mult))
        V(lambda e: e.tensor_tensor(
            tqb_sb[:, 0:width], tq8_sb[:, 0:width],
            s2bt_sb[:, 0:nj].broadcast_to([128, nj, 32]), ALU.mult))
        V(lambda e: e.transpose(dst_bf16_ap, tqb_sb[:, 0:width]))

    WQST = {"tmark": [0, 0], "xev": {0: 0, 1: 0}, "gmp": [0, 0]}

    def emit_wquant_v(K, rowbuf, pari, extra_waits=()):
        nkb = K // 32
        V(lambda e: e.tensor_reduce(
            bmw_sb[:, 0:nkb], rowbuf[:, 0:K].rearrange("p (b k) -> p b k", k=32),
            AX.X, ALU.max, apply_absolute_value=True), waits=extra_waits)
        V(lambda e, pari=pari: e.tensor_scalar(s2w_sb[pari][:, 0:nkb], bmw_sb[:, 0:nkb],
                                    1.0 / 448.0, 1e-30, ALU.mult, ALU.max),
          waits=((s_g, WQST["gmp"][pari]),))
        V(lambda e, pari=pari: e.reciprocal(rsw_sb[:, 0:nkb], s2w_sb[pari][:, 0:nkb]))
        V(lambda e, pari=pari: e.tensor_scalar_mul(s2w_sb[pari][:, 0:nkb], s2w_sb[pari][:, 0:nkb], 2.0))
        sm = V(lambda e, pari=pari: e.scalar_tensor_tensor(
            q8w_sb[pari][:, 0:K], rowbuf[:, 0:K], 0.5,
            rsw_sb[:, 0:nkb].broadcast_to([128, nkb, 32]), ALU.mult, ALU.mult))
        gm = G(lambda e, pari=pari: e.tensor_tensor(
            dqr_sb[pari][:, 0:K], q8w_sb[pari][:, 0:K],
            s2w_sb[pari][:, 0:nkb].broadcast_to([128, nkb, 32]), ALU.mult),
            waits=((s_v, sm), (s_mm, WQST["tmark"][pari])))
        WQST["gmp"][pari] = gm
        return (s_g, gm)

    def emit_wquant_t(K, dqbuf, pari, vmw, mmfence=0):
        evm = {}
        ngrp4 = (K + 511) // 512
        for grp in range(ngrp4):
            bank = XBANK2[grp % 2]
            nin = min(4, K // 128 - grp * 4)
            if grp < 2:
                w = [vmw]
                if WQST["xev"][grp % 2]:
                    w.append((s_sc, WQST["xev"][grp % 2]))
            else:
                w = [(s_sc, evm[grp - 2])]
            for i4 in range(nin):
                cc2 = grp * 4 + i4
                tm = T(lambda e, cc2=cc2, i4=i4, bank=bank, pari=pari: e.transpose(
                    bank[:, i4 * 128:(i4 + 1) * 128],
                    dqr_sb[pari][:, cc2 * 128:(cc2 + 1) * 128], id128_sb[:, :]),
                    waits=w if i4 == 0 else (), mark=(i4 == nin - 1))
                w = ()
            evm[grp] = SC(lambda e, grp=grp, nin=nin, bank=bank: e.copy(
                dqbuf[:, grp * 512:grp * 512 + nin * 128], bank[:, 0:nin * 128]),
                waits=((s_mm, tm),) + (((s_mm, mmfence),) if grp == 0 else ()))
        WQST["tmark"][pari] = tm
        for grp in range(ngrp4):
            WQST["xev"][grp % 2] = evm[grp]
        return evm

    def emit_ln(tok0, ntok, xst, xsq, xn_write, xnf_consumer,
                sumbanks, bcbanks, srow_sb, SEG, first_waits=(), stepped=False):
        ngrp = max(1, ntok // 512)
        gw = ntok // ngrp
        NB = len(xst)
        sq_marks = {}
        mm_marks = {}
        dtok = {}
        res = {}
        steps = []

        def chunk_step(cc):
            w = [(s_mm, mm_marks[cc - NB])] if cc >= NB else list(first_waits)
            dtok[cc] = stx.load(lambda e, cc=cc: e.dma_start(
                xst[cc % NB][:, 0:ntok], xs[cc * 128:(cc + 1) * 128, tok0:tok0 + ntok]),
               waits=w)
            sq_marks[cc] = SC(lambda e, cc=cc: e.activation(
                xsq[cc % 2][:, 0:ntok], xst[cc % NB][:, 0:ntok], AF.Square),
                waits=[dtok[cc]] + ([(s_mm, mm_marks[cc - 2])] if cc >= 2 else []))
            for gq in range(ngrp):
                T(lambda e, cc=cc, gq=gq: e.matmul(
                    sumbanks[gq][0][0:1, 0:gw], onesc_sb[:, :],
                    xst[cc % NB][:, gq * gw:(gq + 1) * gw],
                    start=(cc == 0), stop=(cc == HC - 1)),
                  waits=(dtok[cc],) + (tuple(first_waits) if cc == 0 and gq == 0 else ()))
            for gq in range(ngrp):
                mk = T(lambda e, cc=cc, gq=gq: e.matmul(
                    sumbanks[gq][1][0:1, 0:gw], onesc_sb[:, :],
                    xsq[cc % 2][:, gq * gw:(gq + 1) * gw],
                    start=(cc == 0), stop=(cc == HC - 1)),
                  waits=((s_sc, sq_marks[cc]),), mark=(gq == ngrp - 1))
            mm_marks[cc] = CN[s_mm]

        for cc in range(HC):
            steps.append(lambda cc=cc: chunk_step(cc))

        def finish_step():
            smw = [(s_mm, CN[s_mm])]
            for gq in range(ngrp):
                sgq = slice(gq * gw, (gq + 1) * gw)
                V(lambda e, sgq=sgq, gq=gq: e.tensor_scalar_mul(
                    srow_sb[0:1, sgq], sumbanks[gq][0][0:1, 0:gw], HINV), waits=smw)
                smw = ()
                V(lambda e, sgq=sgq, gq=gq: e.tensor_scalar_mul(
                    srow_sb[0:1, 2 * SEG + gq * gw:2 * SEG + (gq + 1) * gw],
                    sumbanks[gq][1][0:1, 0:gw], HINV))
            V(lambda e: e.tensor_mul(srow_sb[0:1, SEG:SEG + ntok],
                                     srow_sb[0:1, 0:ntok], srow_sb[0:1, 0:ntok]))
            V(lambda e: e.tensor_sub(srow_sb[0:1, 2 * SEG:2 * SEG + ntok],
                                     srow_sb[0:1, 2 * SEG:2 * SEG + ntok], srow_sb[0:1, SEG:SEG + ntok]))
            vm = V(lambda e: e.tensor_scalar_add(srow_sb[0:1, 2 * SEG:2 * SEG + ntok], srow_sb[0:1, 2 * SEG:2 * SEG + ntok], EPS))
            SC(lambda e: e.activation(srow_sb[0:1, 2 * SEG:2 * SEG + ntok], srow_sb[0:1, 2 * SEG:2 * SEG + ntok], AF.Ln),
               waits=((s_v, vm),))
            scm = SC(lambda e: e.activation(srow_sb[0:1, 2 * SEG:2 * SEG + ntok], srow_sb[0:1, 2 * SEG:2 * SEG + ntok],
                                            AF.Exp, scale=-0.5))
            vm = V(lambda e: e.scalar_tensor_tensor(
                srow_sb[0:1, 0:ntok], srow_sb[0:1, 0:ntok], -1.0, srow_sb[0:1, 2 * SEG:2 * SEG + ntok],
                ALU.mult, ALU.mult), waits=((s_sc, scm),))
            for gq in range(ngrp):
                br, bs = bcbanks[gq]
                T(lambda e, gq=gq, br=br: e.matmul(
                    br[:, 0:gw], ones1_sb[:, :], srow_sb[0:1, 2 * SEG + gq * gw:2 * SEG + (gq + 1) * gw],
                    start=True, stop=True), waits=((s_v, vm),))
                T(lambda e, gq=gq, bs=bs: e.matmul(
                    bs[:, 0:gw], ones1_sb[:, :], srow_sb[0:1, gq * gw:(gq + 1) * gw],
                    start=True, stop=True), mark=(gq == ngrp - 1))
            res['bcm'] = CN[s_mm]

        v_marks = {}
        c_marks = {}

        def affine_chunk(cc):
            w = ([(s_v, v_marks[cc - NB])] if cc >= NB
                 else [(s_mm, mm_marks[HC - NB + cc]), (s_sc, sq_marks[HC - 1])])
            dtok[cc] = stx.load(lambda e, cc=cc: e.dma_start(
                xst[cc % NB][:, 0:ntok], xs[cc * 128:(cc + 1) * 128, tok0:tok0 + ntok]),
               waits=w)
            first = [dtok[cc], (s_mm, res['bcm'])]
            for gq in range(ngrp):
                br, bs = bcbanks[gq]
                sgq = slice(gq * gw, (gq + 1) * gw)
                V(lambda e, cc=cc, sgq=sgq, br=br: e.tensor_tensor(
                    xst[cc % NB][:, sgq], xst[cc % NB][:, sgq], br[:, 0:gw], ALU.mult),
                  waits=first)
                first = ()
                V(lambda e, cc=cc, sgq=sgq, bs=bs: e.tensor_tensor(
                    xst[cc % NB][:, sgq], xst[cc % NB][:, sgq], bs[:, 0:gw], ALU.add))
            wdst = xn_write(cc)
            if wdst is not None:
                V(lambda e, cc=cc, wdst=wdst: e.tensor_copy(wdst, xst[cc % NB][:, 0:ntok]))
            c_marks[cc] = CN[s_v]
            if xnf_consumer is not None:
                xnf_consumer(cc, xst[cc % NB])
            v_marks[cc] = CN[s_v]
            if cc == HC - 1:
                res['ready'] = CN[s_v]
                res['v'] = v_marks
                res['c'] = c_marks

        steps.append(finish_step)
        for cc in range(HC):
            steps.append(lambda cc=cc: affine_chunk(cc))
        if stepped:
            return steps, res
        for st in steps:
            st()
        return res['ready'], res['v'], res['c']


    def _early_exit(stacks):
        R.op("sync", lambda e: e.nop(),
             waits=((s_res0, CN[s_res0]), (s_res1, CN[s_res1]),
                    (s_dm, CN[s_dm]), (s_dmg, CN[s_dmg])))
        R.build()
        for st in stacks:
            st.close()
        es.close()
        nc.finalize()
        nc._final_counts = {getattr(s, 'name', str(s)): v for s, v in CN.items()}
        return nc, c

    esA = ExitStack()
    asb = lambda n, shp, dt: esA.enter_context(nc.sbuf_tensor(n, list(shp), dt))
    xdq_sb = asb("xdq", [128, HC * TQ], BF16)
    xstA = [asb(f"xstA{i}", [128, TQ], F32) for i in range(2)]
    srowA_sb = asb("srowA", [1, 3 * TQ], F32)

    esA1 = ExitStack()
    xn_sb = esA1.enter_context(nc.sbuf_tensor("xn", [128, HC * TQ], BF16))
    xsqA = [esA1.enter_context(nc.sbuf_tensor(f"xsqA{i}", [128, TQ], F32)) for i in range(2)]
    qsqb_sb = [esA1.enter_context(nc.sbuf_tensor(f"qsqb{i}", [128, TQ], BF16)) for i in range(2)]
    wqb_sb = [esA1.enter_context(nc.sbuf_tensor(f"wqb{i}", [128, H], BF16)) for i in range(2)]

    def a_consume(cc, buf):
        w5 = min(512, TQ)
        for gq in range(TQ // w5):
            emit_quantT(buf[:, gq * w5:(gq + 1) * w5],
                        xdq_sb[:, cc * TQ + gq * w5:cc * TQ + (gq + 1) * w5], width=w5)

    _, xnmarks, xncop = emit_ln(S - TQ, TQ, xstA, xsqA,
            lambda cc: xn_sb[:, cc * TQ:(cc + 1) * TQ],
            a_consume,
            sumbanks=[(PS_X0, PS_X1), (PS_S0, PS_S1)],
            bcbanks=[(PS_O0, PS_O1), (PS_X0, PS_X1)], srow_sb=srowA_sb, SEG=TQ)
    if PHASES == 11:
        return _early_exit([esA1, esA])

    evac_marks = {}
    for h in range(NH):
        w = [(s_v, evac_marks[(h - 2, NQG - 1)])] if h >= 2 else []
        wtok = swc.load(lambda e, h=h: e.dma_start(wqb_sb[h % 2][:, 0:H], wq_cs[h]), waits=w)
        waits = [wtok]
        if h >= 1:
            waits.append((s_v, evac_marks[(h - 1, NQG - 1)]))
        mks = {}
        for k in range(HC):
            for gq in range(NQG):
                bank = SBANK[gq]
                xw2 = list(waits) if (k == 0 and gq == 0) else []
                if h == 0 and gq == 0:
                    xw2.append((s_v, xncop[k]))
                mks[gq] = T(lambda e, h=h, gq=gq, k=k, bank=bank: e.matmul(
                    bank[:, 0:QGW], wqb_sb[h % 2][:, k * 128:(k + 1) * 128],
                    xn_sb[:, k * TQ + gq * QGW:k * TQ + (gq + 1) * QGW],
                    start=(k == 0), stop=(k == HC - 1)),
                  waits=xw2, mark=(k == HC - 1))
        for gq in range(NQG):
            bank = SBANK[gq]
            evac_marks[(h, gq)] = V(lambda e, h=h, gq=gq, bank=bank: e.tensor_scalar_add(
                Q_sb[:, h * TQ + gq * QGW:h * TQ + (gq + 1) * QGW],
                bank[:, 0:QGW], bq_sb[:, h:h + 1]), waits=((s_mm, mks[gq]),))
    R.barrier()

    sqm = {}
    qtm = {}
    for h in range(NH):
        sqm[h] = V(lambda e, h=h: e.tensor_mul(
            qsqb_sb[h % 2][:, 0:TQ], Q_sb[:, h * TQ:(h + 1) * TQ], Q_sb[:, h * TQ:(h + 1) * TQ]),
            waits=[(s_mm, qtm[h - 2])] if h >= 2 else ())
        for gq in range(NQG):
            mk = T(lambda e, h=h, gq=gq: e.matmul(
                [PS_X0, PS_X1][gq][0:NH, 0:QGW], indb_sb[:, h * NH:h * NH + NH],
                qsqb_sb[h % 2][:, gq * QGW:(gq + 1) * QGW],
                start=(h == 0), stop=(h == NH - 1)),
              waits=((s_v, sqm[h]),), mark=True)
        qtm[h] = mk
    V(lambda e: e.reduce_max(stc(C_T0), PS_X0[0:NH, 0:QGW], AX.X), waits=((s_mm, mk),))
    if NQG > 1:
        V(lambda e: e.reduce_max(stc(C_T1), PS_X1[0:NH, 0:QGW], AX.X))
        V(lambda e: e.tensor_max(stc(C_T0), stc(C_T0), stc(C_T1)))
    vm = CN[s_v]
    SC(lambda e: e.activation(stc(C_T0), stc(C_T0), AF.Ln), waits=((s_v, vm),))
    SC(lambda e: e.activation(stc(C_QMAX), stc(C_T0), AF.Exp, scale=0.5))
    R.barrier()
    if PHASES == 12:
        return _early_exit([esA1, esA])
    esA1.close()

    esA2 = ExitStack()
    hh_sb = esA2.enter_context(nc.sbuf_tensor("hh", [128, NHB * TQ], BF16))
    q8w_sb = [esA2.enter_context(nc.sbuf_tensor(f"q8w{i}", [128, 2048], FP8)) for i in range(2)]
    bmw_sb = esA2.enter_context(nc.sbuf_tensor("bmw", [128, 128], F32))
    rsw_sb = esA2.enter_context(nc.sbuf_tensor("rsw", [128, 128], F32))
    s2w_sb = [esA2.enter_context(nc.sbuf_tensor(f"s2w{i}", [128, 128], F32)) for i in range(2)]
    dqr_sb = [esA2.enter_context(nc.sbuf_tensor(f"dqr{i}", [128, 2048], F32)) for i in range(2)]
    wrow_sb = [esA2.enter_context(nc.sbuf_tensor(f"wrow{i}", [128, DQ], F32)) for i in range(2)]
    dqw_sb = [esA2.enter_context(nc.sbuf_tensor(f"dqw{i}", [128, DQ], BF16)) for i in range(2)]
    sg_sb = esA2.enter_context(nc.sbuf_tensor("sg", [128, 2 * QGW], F32))
    hf_sb = esA2.enter_context(nc.sbuf_tensor("hf", [128, QGW], F32))
    dtl_sb = [esA2.enter_context(nc.sbuf_tensor(f"dtl{i}", [128, TQ], F32)) for i in range(2)]

    silu_mark = {}
    hf_mark = {}
    half_fence = {"v": 0}
    prevh = {"duse": None, "dmms": None}
    for half in range(NCH):
        use_marks = {}
        gate_mm = {}
        last_mm = {}
        evall = {}
        hh_ready = {}

        def emit_guw(j, half=half):
            ib = half * NHB + j // 2
            par = j % 2
            evmj = evall[j]
            lastg = -1
            mk = 0
            for k in range(HC):
                for gq in range(NQG):
                    bank = [SBANK, [PS_O0, PS_O1]][par][gq]
                    waits = []
                    if k == 0:
                        if par == 0 and gq in silu_mark:
                            waits.append((s_sc, silu_mark[gq]))
                        if par == 1 and gq in hf_mark:
                            waits.append((s_v, hf_mark[gq]))
                        if j == 0 and gq == 0 and half_fence["v"]:
                            waits.append((s_v, half_fence["v"]))
                    if gq == 0 and k // 4 > lastg:
                        lastg = k // 4
                        waits.append((s_sc, evmj[lastg]))
                    mk = T(lambda e, par=par, gq=gq, k=k, bank=bank: e.matmul(
                        bank[:, 0:QGW], dqw_sb[par][:, k * 128:(k + 1) * 128],
                        xdq_sb[:, k * TQ + gq * QGW:k * TQ + (gq + 1) * QGW],
                        start=(k == 0), stop=(k == HC - 1)),
                      waits=waits, mark=(k == HC - 1))
                    if k == HC - 1:
                        if par == 0:
                            gate_mm[(ib, gq)] = mk
                        last_mm[j] = mk
            if par == 1 and FFNLITE >= 3:
                for gq in range(NQG):
                    gb = SBANK[gq]
                    ub = [PS_O0, PS_O1][gq]
                    sw2 = [(s_mm, gate_mm[(ib, gq)])]
                    if gq in hf_mark:
                        sw2.append((s_v, hf_mark[gq]))
                    scm = SC(lambda e, gb=gb, gq=gq: e.activation(
                        sg_sb[:, gq * QGW:(gq + 1) * QGW], gb[:, 0:QGW], AF.Silu),
                             waits=sw2)
                    silu_mark[gq] = scm
                    hf_mark[gq] = V(lambda e, ub=ub, gq=gq: e.tensor_tensor(
                        hf_sb[:, 0:QGW], sg_sb[:, gq * QGW:(gq + 1) * QGW],
                        ub[:, 0:QGW], ALU.mult),
                      waits=((s_sc, scm), (s_mm, mk)))
                    wdt = min(512, QGW)
                    for sg2 in range(QGW // wdt):
                        loc = (ib % NHB) * TQ + gq * QGW + sg2 * wdt
                        emit_quantT(hf_sb[:, sg2 * wdt:(sg2 + 1) * wdt],
                                    hh_sb[:, loc:loc + wdt], width=wdt)
                hh_ready[ib % NHB] = CN[s_v]

        for j in range(2 * NHB):
            ib = half * NHB + j // 2
            par = j % 2
            if j >= 2:
                w = [(s_v, use_marks[j - 2])]
            elif prevh["duse"]:
                w = [(s_v, prevh["duse"][j])]
            else:
                w = []
            rtok = strow.load(lambda e, ib=ib, par=par, j=j: e.dma_start(
                wrow_sb[par][:, 0:H], (wg_rows if j % 2 == 0 else wu_rows)[ib]), waits=w)
            xw = [rtok]
            mmf = last_mm.get(j - 2, 0)
            if j < 2 and prevh["dmms"]:
                mmf = prevh["dmms"][j]
            if mmf:
                xw.append((s_mm, mmf))
            vmj = emit_wquant_v(H, wrow_sb[par], par, extra_waits=tuple(xw))
            use_marks[j] = CN[s_v]
            if FFNLITE >= 2 and j >= 1:
                emit_guw(j - 1)
            evall[j] = emit_wquant_t(H, dqw_sb[par], par, vmj, mmfence=mmf)
        duse = {}
        dmms = {}
        evmk = {}
        stmk = {}
        mk_gq = {}
        evallD = {}
        def dprequant(jj):
            par = jj % 2
            w = [(s_v, use_marks[2 * NHB - 2 + jj])]
            k0 = half * FFHALF
            rtok = strow.load(lambda e, jj=jj, k0=k0, par=par: e.dma_start(
                wrow_sb[par][:, 0:DQ], wd_rows[jj][:, k0:k0 + DQ]), waits=w)
            mmf = last_mm[2 * NHB - 2 + jj]
            vmj = emit_wquant_v(DQ, wrow_sb[par], par,
                                extra_waits=(rtok, (s_mm, mmf)))
            duse[jj] = CN[s_v]
            evallD[jj] = emit_wquant_t(DQ, dqw_sb[par], par, vmj, mmfence=mmf)

        dprequant(0)
        if FFNLITE >= 2:
            emit_guw(2 * NHB - 1)
        guw_sc = CN[s_sc]
        dprequant(1)
        if PHASES == 13:
            R.barrier()
            return _early_exit([esA2, esA])

        def emit_devac(m, half=half):
            if half > 0:
                dtok1 = stm.load(lambda e, m=m: e.dma_start(
                    dtl_sb[m % 2][:, 0:TQ], dacc_hbm[m * 128:(m + 1) * 128, :]),
                   waits=((s_res0, CN[s_res0]), (s_res1, CN[s_res1])) if m <= 1 else
                         [stmk[m - 2]])
            if half == NCH - 1:
                dtok2 = stm.load(lambda e, m=m: e.dma_start(
                    xstA[m % 2][:, 0:TQ], xs[m * 128:(m + 1) * 128, S - TQ:S]))
            for gq in range(NQG):
                bank = SBANK[(m * NQG + gq) % 2]
                sw = [stmk[m - 2]] if m >= 2 else []
                if half == 0:
                    V(lambda e, m=m, gq=gq, bank=bank: e.tensor_copy(
                        dtl_sb[m % 2][:, gq * QGW:(gq + 1) * QGW], bank[:, 0:QGW]),
                      waits=[(s_mm, mk_gq[gq])] + sw)
                else:
                    V(lambda e, m=m, gq=gq, bank=bank: e.tensor_tensor(
                        dtl_sb[m % 2][:, gq * QGW:(gq + 1) * QGW],
                        dtl_sb[m % 2][:, gq * QGW:(gq + 1) * QGW],
                        bank[:, 0:QGW], ALU.add),
                      waits=[(s_mm, mk_gq[gq]), dtok1] + sw)
                    if half == NCH - 1:
                        V(lambda e, m=m, gq=gq: e.tensor_tensor(
                            dtl_sb[m % 2][:, gq * QGW:(gq + 1) * QGW],
                            dtl_sb[m % 2][:, gq * QGW:(gq + 1) * QGW],
                            xstA[m % 2][:, gq * QGW:(gq + 1) * QGW], ALU.add),
                          waits=(dtok2,))
            evmk[m] = vm2 = CN[s_v]
            dst = dacc_hbm if half < NCH - 1 else resid_hbm
            stmk[m] = (s_res[m % 2], R.op("sync", lambda e, m=m, dst=dst: e.dma_start(
                dst[m * 128:(m + 1) * 128, :], dtl_sb[m % 2][:, 0:TQ]),
                waits=((s_v, vm2),), incs=((s_res[m % 2], 16),)))

        def emit_dmm(jj):
            m, qtr = jj // NDQ, jj % NDQ
            par = jj % 2
            evmj = evallD[jj]
            nk = DQ // 128
            lastg = -1
            for k in range(nk):
                kg = qtr * nk + k
                for gq in range(NQG):
                    bank = SBANK[(m * NQG + gq) % 2]
                    waits = []
                    if k == 0 and gq == 0 and qtr == 0:
                        prev = m - (2 if NQG == 1 else 1)
                        if prev >= 0:
                            waits.append((s_v, evmk[prev]))
                        else:
                            waits.append((s_sc, guw_sc))
                            if half_fence["v"]:
                                waits.append((s_v, half_fence["v"]))
                    if gq == 0:
                        waits.append((s_v, hh_ready[kg]))
                    if gq == 0 and k // 4 > lastg:
                        lastg = k // 4
                        waits.append((s_sc, evmj[lastg]))
                    mk = T(lambda e, par=par, gq=gq, k=k, kg=kg, bank=bank: e.matmul(
                        bank[:, 0:QGW], dqw_sb[par][:, k * 128:(k + 1) * 128],
                        hh_sb[:, kg * TQ + gq * QGW:kg * TQ + (gq + 1) * QGW],
                        start=(kg == 0), stop=(kg == NHB - 1)),
                      waits=waits, mark=(kg == NHB - 1))
                    mk_gq[gq] = mk
            dmms[jj] = CN[s_mm]
            if qtr == NDQ - 1:
                emit_devac(m)

        emit_dmm(0)
        for jj in range(2, HC * NDQ):
            m, qtr = jj // NDQ, jj % NDQ
            par = jj % 2
            w = [(s_v, duse[jj - 2])]
            k0 = half * FFHALF + qtr * DQ
            rtok = strow.load(lambda e, m=m, k0=k0, par=par: e.dma_start(
                wrow_sb[par][:, 0:DQ], wd_rows[m][:, k0:k0 + DQ]), waits=w)
            xw = [rtok]
            mmf = dmms.get(jj - 2, 0)
            if mmf:
                xw.append((s_mm, mmf))
            vmj = emit_wquant_v(DQ, wrow_sb[par], par, extra_waits=tuple(xw))
            duse[jj] = CN[s_v]
            emit_dmm(jj - 1)
            evallD[jj] = emit_wquant_t(DQ, dqw_sb[par], par, vmj, mmfence=mmf)
        emit_dmm(HC * NDQ - 1)
        half_fence["v"] = CN[s_v]
        prevh["duse"] = (duse[HC * NDQ - 2], duse[HC * NDQ - 1])
        prevh["dmms"] = (dmms[HC * NDQ - 2], dmms[HC * NDQ - 1])
        if PHASES == 14:
            R.barrier()
            return _early_exit([esA2, esA])
    esA2.close()
    esA.close()

    esB = ExitStack()
    bsb = lambda n, shp, dt: esB.enter_context(nc.sbuf_tensor(n, list(shp), dt))
    O_sb = bsb("O", [128, NH * TQ], F32)
    bvbc_sb = bsb("bvbc", [128, H], BF16)
    K_sb = bsb("K", [128, NH * MEGA], BF16)
    V_sb = bsb("V", [128, KSUB * H], BF16)
    xnm_sb = bsb("xnm", [128, HC * MEGA], BF16)
    wvb_sb = bsb("wvb", [128, (HC // 2) * VFW], BF16)
    wkb_sb = [bsb(f"wkb{i}", [128, H], BF16) for i in range(2)]
    PT_sb = [bsb(f"PT{i}", [128, QGW], BF16) for i in range(4)]
    ksq_sb = bsb("ksq", [128, MEGA], BF16)
    xstB = [bsb(f"xstB{i}", [128, MEGA], F32) for i in range(4)]
    xsqB = [bsb(f"xsqB{i}", [128, MEGA], F32) for i in range(2)]
    srowB_sb = bsb("srowB", [1, 3 * MEGA], F32)

    if PHASES < 2:
        R.op("sync", lambda e: e.nop(),
             waits=((s_res0, CN[s_res0]), (s_res1, CN[s_res1])))
        R.build()
        esB.close()
        es.close()
        nc.finalize()
        nc._final_counts = {getattr(s, 'name', str(s)): v for s, v in CN.items()}
        return nc, c
    resw = ((s_res0, CN[s_res0]), (s_res1, CN[s_res1]))
    R.op("sync", lambda e: e.nop(), waits=resw)
    R.op("gpsimd", lambda e: e.nop(), waits=resw)
    R.op("gpsimd", lambda e: e.dma_start(bvbc_sb[:, :], bv_bc), incs=((s_dmg, 16),))
    bvtok = (s_dmg, CN[s_dmg])
    T(lambda e: e.nop(), waits=resw)
    SC(lambda e: e.nop(), waits=resw)
    V(lambda e: e.memset(O_sb[:, :], 0.0), waits=resw)
    V(lambda e: e.memset(Lacc_sb[:, :], 0.0))
    R.barrier()

    xnm_ready = {}
    xnm_ready[0], _, _ = emit_ln(0, MEGA, xstB, xsqB,
            lambda cc: xnm_sb[:, cc * MEGA:(cc + 1) * MEGA],
            None,
            sumbanks=[(PS_X0, PS_X1)],
            bcbanks=[(PS_X0, PS_X1)], srow_sb=srowB_sb, SEG=MEGA)
    R.barrier()
    for M in range(NMEGA):
        kevac = {}
        kmmk = {}
        for h in range(NH):
            if h < 2 and M > 0:
                ktok = kpre[h]
            else:
                w = [(s_v, kevac[h - 2])] if h >= 2 else []
                ktok = swc.load(lambda e, h=h: e.dma_start(wkb_sb[h % 2][:, 0:H], wk_cs[h]), waits=w)
            bank = SBANK[h % 2]
            waits = [ktok]
            if h == 0:
                waits.append((s_v, xnm_ready[M]))
            if h >= 2:
                waits.append((s_v, kevac[h - 2]))
            for k in range(HC):
                mk = T(lambda e, h=h, k=k, bank=bank: e.matmul(
                    bank[:, 0:MEGA], wkb_sb[h % 2][:, k * 128:(k + 1) * 128],
                    xnm_sb[:, k * MEGA:(k + 1) * MEGA],
                    start=(k == 0), stop=(k == HC - 1)),
                  waits=waits if k == 0 else (), mark=(k == HC - 1))
            kmmk[h] = mk
            V(lambda e, h=h, bank=bank: e.tensor_scalar_add(
                K_sb[:, h * MEGA:(h + 1) * MEGA], bank[:, 0:MEGA], bq_sb[:, HC + h:HC + h + 1]),
              waits=((s_mm, mk),))
            kevac[h] = V(lambda e, h=h: e.tensor_mul(
                ksq_sb[:, 0:MEGA], K_sb[:, h * MEGA:(h + 1) * MEGA],
                K_sb[:, h * MEGA:(h + 1) * MEGA]))
            mk = T(lambda e, h=h: e.matmul(
                PS_O0[0:NH, 0:MEGA], indb_sb[:, h * NH:h * NH + NH], ksq_sb[:, 0:MEGA],
                start=(h == 0), stop=(h == NH - 1)),
              waits=((s_v, kevac[h]),), mark=(h == NH - 1))
        if DBG:
            R.op("sync", lambda e, M=M: e.dma_start(
                dbgK[M * 128:(M + 1) * 128, :], K_sb[:, :]),
                waits=((s_v, CN[s_v]),), incs=((s_dm, 16),))
        V(lambda e: e.reduce_max(stc(C_T0), PS_O0[0:NH, 0:MEGA], AX.X), waits=((s_mm, mk),))
        vm = CN[s_v]
        knread = vm
        SC(lambda e: e.activation(stc(C_T0), stc(C_T0), AF.Ln), waits=((s_v, vm),))
        scm = SC(lambda e: e.activation(stc(C_KMAX), stc(C_T0), AF.Exp, scale=0.5))
        V(lambda e: e.tensor_mul(stc(C_T1), stc(C_QMAX), stc(C_KMAX)), waits=((s_sc, scm),))
        V(lambda e: e.tensor_scalar_mul(stc(C_T1), stc(C_T1), SCALE))
        if M == 0:
            V(lambda e: e.tensor_copy(stc(C_SB), stc(C_T1)))
        else:
            V(lambda e: e.tensor_max(stc(C_SBN), stc(C_SB), stc(C_T1)))
            vm = V(lambda e: e.tensor_sub(stc(C_T2), stc(C_SB), stc(C_SBN)))
            scm = SC(lambda e: e.activation(stc(C_FAC), stc(C_T2), AF.Exp),
                     waits=((s_v, vm),))
            V(lambda e: e.tensor_copy(stc(C_SB), stc(C_SBN)), waits=((s_sc, scm),))
        V(lambda e: e.transpose(rowt_sb[:, :], stat_sb[0:32, C_SB:C_SB + 32]))
        V(lambda e: e.tensor_scalar(biasrow_sb[0:1, 96:96 + NH], rowt_sb[0:1, 0:NH],
                                    -1.0, CUSH, ALU.mult, ALU.add))
        for ks in range(KSUB):
            V(lambda e, ks=ks, M=M: e.tensor_scalar_add(
                biasrow_sb[0:1, ks * NH:(ks + 1) * NH],
                biasrow_sb[0:1, 96:96 + NH],
                flag_sb[0:1, M * KSUB + ks:M * KSUB + ks + 1]))
        vm = CN[s_v]
        tmB = T(lambda e: e.matmul(PS_S0[:, 0:NH * KSUB], ones1_sb[:, :],
                                   biasrow_sb[0:1, 0:NH * KSUB], start=True, stop=True),
                waits=((s_v, vm),), mark=True)
        V(lambda e: e.tensor_copy(bias_sb[:, 0:NH * KSUB], PS_S0[:, 0:NH * KSUB]),
          waits=((s_mm, tmB),))
        bias_ready = CN[s_v]
        if M > 0:
            vm = V(lambda e: e.transpose(rowt_sb[:, :], stat_sb[0:32, C_FAC:C_FAC + 32]),
                   waits=((s_mm, tmB),))
            tm = T(lambda e: e.matmul(PS_S1[:, 0:NH], ones1_sb[:, :],
                                      rowt_sb[0:1, 0:NH], start=True, stop=True),
                   waits=((s_v, vm),), mark=True)
            V(lambda e: e.tensor_copy(facbc_sb[:, :], PS_S1[:, 0:NH]), waits=((s_mm, tm),))
            for g in range(NQG):
                V(lambda e, g=g: e.tensor_scalar_mul(
                    Lacc_sb[0:NH, g * QGW:(g + 1) * QGW],
                    Lacc_sb[0:NH, g * QGW:(g + 1) * QGW], stc(C_FAC)),
                  waits=((s_dm, CN[s_dm]),))
        vevac = {}
        vln = {}
        nhalf = HC // 2 if HC > 1 else 1
        for F4 in range(VFG):
            for sub in range(2 if HC > 1 else 1):
                idx = F4 * 2 + sub if HC > 1 else F4
                w = [(s_mm, vln[idx - 1])] if idx >= 1 else []
                if idx >= 1:
                    w.append((s_v, vevac[idx - 1]))
                vtok = swv.load(lambda e, F4=F4, c0=sub * nhalf: e.dma_start(
                    wvb_sb[:, 0:nhalf * VFW],
                    wv_rs[F4][:, c0 * VFW:(c0 + nhalf) * VFW]), waits=w)
                c0 = sub * nhalf
                for ks in range(KSUB):
                    bank = [PS_O0, PS_O1, PS_X0, PS_X1][ks]
                    for ci in range(nhalf):
                        k = c0 + ci
                        vw = [vtok] if ci == 0 else []
                        if idx == 0 and ks == 0 and ci == 0:
                            vw.append((s_v, knread))
                        mk = T(lambda e, k=k, ci=ci, ks=ks, bank=bank: e.matmul(
                            bank[:, 0:VFW], xnm_sb[:, k * MEGA + ks * 128:k * MEGA + ks * 128 + 128],
                            wvb_sb[:, ci * VFW:(ci + 1) * VFW],
                            start=(k == 0), stop=(k == HC - 1)),
                          waits=vw,
                          mark=(ci == nhalf - 1))
                    if sub == (1 if HC > 1 else 0):
                        V(lambda e, F4=F4, ks=ks, bank=bank: e.scalar_tensor_tensor(
                            V_sb[:, ks * H + F4 * VFW:ks * H + (F4 + 1) * VFW],
                            bank[:, 0:VFW], 1.0,
                            bvbc_sb[:, F4 * VFW:(F4 + 1) * VFW], ALU.mult, ALU.add),
                          waits=((s_mm, mk), bvtok))
                vevac[idx] = CN[s_v]
                vln[idx] = CN[s_mm]
        vready = CN[s_v]
        lnsteps = []
        if M + 1 < NMEGA:
            lnsteps, lnres = emit_ln(
                (M + 1) * MEGA, MEGA, xstB, xsqB,
                lambda cc: xnm_sb[:, cc * MEGA:(cc + 1) * MEGA],
                None,
                sumbanks=[(PS_X0, PS_X1)],
                bcbanks=[(PS_X0, PS_X1)], srow_sb=srowB_sb, SEG=MEGA,
                first_waits=((s_v, vready),), stepped=True)
            ln_per = -(-len(lnsteps) // max(1, NH - 1))
        exp_marks = {}
        pv_marks = {}
        exp_pt = {}
        oevac = {}
        lmk = {}
        mkq_all = {}

        def emit_qk(h, ks, g):
            if ks == 0 and g == 0 and M > 0:
                for g2 in range(NQG):
                    V(lambda e, h=h, g2=g2: e.tensor_scalar_mul(
                        O_sb[:, h * TQ + g2 * QGW:h * TQ + (g2 + 1) * QGW],
                        O_sb[:, h * TQ + g2 * QGW:h * TQ + (g2 + 1) * QGW],
                        facbc_sb[:, h:h + 1]))
            sidx = (ks * NQG + g) % 2
            sbank = SBANK[sidx]
            waits = [(s_v, vready)]
            if sidx in exp_marks:
                waits.append((s_sc, exp_marks[sidx]))
            mkq_all[(h, ks, g)] = T(lambda e, h=h, g=g, ks=ks, sbank=sbank: e.matmul(
                sbank[:, 0:QGW], K_sb[:, h * MEGA + ks * 128:h * MEGA + ks * 128 + 128],
                Q_sb[:, h * TQ + g * QGW:h * TQ + (g + 1) * QGW],
                start=True, stop=True),
              waits=waits, mark=True)

        def emit_rest(h, ks, g):
            kslot = M * KSUB + ks
            kpar = ks % 2
            j = kslot - OWN0
            mkq = mkq_all[(h, ks, g)]
            bidx = kpar * 2 + g
            sidx = (ks * NQG + g) % 2
            sbank = SBANK[sidx]
            trim = None
            if j >= 0:
                for qq in range(QSUBG):
                    if j == g * QSUBG + qq:
                        trim = V(lambda e, qq=qq, sbank=sbank: e.tensor_tensor(
                            sbank[:, qq * 128:(qq + 1) * 128],
                            sbank[:, qq * 128:(qq + 1) * 128],
                            tri_sb[:, :], ALU.add), waits=((s_mm, mkq),))
            ew = [(s_mm, mkq), (s_v, max(bias_ready, trim or 0))]
            if (kpar, g) in pv_marks:
                ew.append((s_mm, pv_marks[(kpar, g)]))
            em = SC(lambda e, ks=ks, h=h, sbank=sbank, bidx=bidx: e.activation(
                PT_sb[bidx][:, 0:QGW], sbank[:, 0:QGW], AF.Exp,
                bias=bias_sb[:, ks * NH + h:ks * NH + h + 1], scale=SCALE),
                waits=ew)
            exp_marks[sidx] = em
            exp_pt[bidx] = em
            memm = None
            if j >= 0:
                for qq in range(QSUBG):
                    if j > g * QSUBG + qq:
                        memm = V(lambda e, qq=qq, bidx=bidx: e.memset(
                            PT_sb[bidx][:, qq * 128:(qq + 1) * 128], 0.0),
                            waits=((s_sc, em),))
            obank = [PS_O0, PS_O1][g]
            pvw = [(s_sc, em)]
            if memm:
                pvw.append((s_v, memm))
            if ks == 0 and (h - 1, g) in oevac:
                pvw.append((s_v, oevac[(h - 1, g)]))
            if ks == 0 and h == 0:
                pvw.append((s_v, vready))
            T(lambda e, ks=ks, h=h, bidx=bidx, obank=obank: e.matmul(
                obank[:, 0:QGW], V_sb[:, ks * H + h * 128:ks * H + h * 128 + 128],
                PT_sb[bidx][:, 0:QGW], start=(ks == 0), stop=(ks == KSUB - 1)),
              waits=pvw)
            mk2 = T(lambda e, ks=ks, h=h, g=g, bidx=bidx: e.matmul(
                LBANK[g][0:NH, 0:QGW], indb_sb[:, h * NH:h * NH + NH],
                PT_sb[bidx][:, 0:QGW],
                start=(h == 0 and ks == 0),
                stop=(h == NH - 1 and ks == KSUB - 1)),
              mark=True)
            pv_marks[(kpar, g)] = mk2
            lmk[g] = mk2
            if ks == KSUB - 1 and g == NQG - 1:
                for g2 in range(NQG):
                    obank = [PS_O0, PS_O1][g2]
                    oevac[(h, g2)] = V(lambda e, h=h, g2=g2, obank=obank: e.tensor_tensor(
                        O_sb[:, h * TQ + g2 * QGW:h * TQ + (g2 + 1) * QGW],
                        O_sb[:, h * TQ + g2 * QGW:h * TQ + (g2 + 1) * QGW],
                        obank[:, 0:QGW], ALU.add), waits=((s_mm, lmk[g2]),))
                if lnsteps:
                    for st in lnsteps[:ln_per]:
                        st()
                    del lnsteps[:ln_per]

        apos = [(h, ks, g) for h in range(NH) for ks in range(KSUB)
                for g in range(NQG)]
        for bi, pos in enumerate(apos):
            emit_qk(*pos)
            if bi >= 1:
                emit_rest(*apos[bi - 1])
        emit_rest(*apos[-1])
        for g in range(NQG):
            V(lambda e, g=g: e.tensor_tensor(
                Lacc_sb[0:NH, g * QGW:(g + 1) * QGW],
                Lacc_sb[0:NH, g * QGW:(g + 1) * QGW],
                LBANK[g][0:NH, 0:QGW], ALU.add), waits=((s_mm, lmk[g]),))
        if M + 1 < NMEGA:
            assert not lnsteps
            xnm_ready[M + 1] = lnres["ready"]
        if M + 1 < NMEGA:
            kpre = [swc.load(lambda e, hh2=hh2: e.dma_start(
                        wkb_sb[hh2][:, 0:H], wk_cs[hh2]),
                        waits=[(s_mm, kmmk[NH - 2 + hh2])])
                    for hh2 in range(2)]
        if DBG:
            R.op("sync", lambda e, M=M: e.dma_start(
                dbgL[M * 16:M * 16 + 16, :], Lacc_sb[:, :]),
                waits=((s_v, CN[s_v]),), incs=((s_dm, 16),))
        R.barrier()
    esB.close()
    if PHASES < 3:
        R.op("sync", lambda e: e.nop(),
             waits=((s_res0, CN[s_res0]), (s_res1, CN[s_res1]), (s_dm, CN[s_dm]), (s_dmg, CN[s_dmg])))
        R.build()
        es.close()
        nc.finalize()
        nc._final_counts = {getattr(s, 'name', str(s)): v for s, v in CN.items()}
        return nc, c

    esC = ExitStack()
    csb = lambda n, shp, dt: esC.enter_context(nc.sbuf_tensor(n, list(shp), dt))
    Obf_sb = csb("Obf", [128, NH * TQ], BF16)
    wob_sb = [csb(f"wob{i}", [128, H], BF16) for i in range(2)]
    ytl_sb = [csb(f"ytl{i}", [128, QGW], F32) for i in range(2)]
    rtl_sb = [csb(f"rtl{i}", [128, TQ], F32) for i in range(2)]

    for g in range(NQG):
        V(lambda e, g=g: e.reciprocal(rL_sb[0:NH, g * QGW:(g + 1) * QGW],
                                      Lacc_sb[0:NH, g * QGW:(g + 1) * QGW]))
    vm = CN[s_v]
    srl2 = Dstream("srl2", "gpsimd")
    bcmk = {}
    for h in range(NH):
        for g in range(NQG):
            i = h * NQG + g
            slot = i % 2
            bank = [PS_X0, PS_X1][i % 2]
            w = [(s_v, vm)]
            if i >= 2:
                w.append((s_mm, bcmk[i - 2]))
            ltok = srl2.load(lambda e, h=h, g=g, slot=slot: e.dma_start(
                rl0_sb[0:1, slot * QGW:(slot + 1) * QGW],
                rL_sb[h:h + 1, g * QGW:(g + 1) * QGW]), waits=w)
            w2 = [ltok]
            if i >= 2:
                w2.append((s_v, CN[s_v]))
            tm = T(lambda e, slot=slot, bank=bank: e.matmul(
                bank[:, 0:QGW], ones1_sb[:, :],
                rl0_sb[0:1, slot * QGW:(slot + 1) * QGW],
                start=True, stop=True), waits=w2, mark=True)
            bcmk[i] = tm
            V(lambda e, h=h, g=g, bank=bank: e.tensor_tensor(
                Obf_sb[:, h * TQ + g * QGW:h * TQ + (g + 1) * QGW],
                O_sb[:, h * TQ + g * QGW:h * TQ + (g + 1) * QGW],
                bank[:, 0:QGW], ALU.mult), waits=((s_mm, tm),))
    R.barrier()
    yev = {}
    otmk = {}
    for m in range(HC):
        w = [(s_v, yev[(m - 2, NQG - 1)])] if m >= 2 else []
        wtok = swc.load(lambda e, m=m: e.dma_start(wob_sb[m % 2][:, 0:H], wo_cs[m]), waits=w)
        ow = [otmk[m - 2]] if m >= 2 else []
        rtok = stm.load(lambda e, m=m: e.dma_start(rtl_sb[m % 2][:, 0:TQ],
                                                   resid_hbm[m * 128:(m + 1) * 128, :]),
                        waits=[(s_res0, CN[s_res0]), (s_res1, CN[s_res1])] + w + ow)
        waits = [wtok]
        if m >= 1:
            waits.append((s_v, yev[(m - 1, NQG - 1)]))
        mks = {}
        for k in range(HC):
            for gq in range(NQG):
                bank = SBANK[gq]
                mks[gq] = T(lambda e, m=m, gq=gq, k=k, bank=bank: e.matmul(
                    bank[:, 0:QGW], wob_sb[m % 2][:, k * 128:(k + 1) * 128],
                    Obf_sb[:, k * TQ + gq * QGW:k * TQ + (gq + 1) * QGW],
                    start=(k == 0), stop=(k == HC - 1)),
                  waits=waits if (k == 0 and gq == 0) else (), mark=(k == HC - 1))
        for gq in range(NQG):
            bank = SBANK[gq]
            V(lambda e, m=m, gq=gq, bank=bank: e.tensor_scalar_add(
                ytl_sb[gq % 2][:, 0:QGW], bank[:, 0:QGW], bo_sb[:, m:m + 1]),
              waits=((s_mm, mks[gq]), rtok))
            yev[(m, gq)] = V(lambda e, m=m, gq=gq: e.tensor_tensor(
                rtl_sb[m % 2][:, gq * QGW:(gq + 1) * QGW],
                rtl_sb[m % 2][:, gq * QGW:(gq + 1) * QGW],
                ytl_sb[gq % 2][:, 0:QGW], ALU.add))
        vm2 = CN[s_v]
        otmk[m] = (s_out[m % 2], R.op("sync", lambda e, m=m: e.dma_start(
            outT[m * 128:(m + 1) * 128, :], rtl_sb[m % 2][:, 0:TQ]),
            waits=((s_v, vm2),), incs=((s_out[m % 2], 16),)))
    R.op("sync", lambda e: e.nop(),
         waits=((s_out0, CN[s_out0]), (s_out1, CN[s_out1]), (s_dm, CN[s_dm])))

    R.build()
    esC.close()
    es.close()
    nc.finalize()
    nc._final_counts = {getattr(s, 'name', str(s)): v for s, v in CN.items()}
    return nc, c


def stage_inputs(cfg, x, Wqkv, bqkv, Wout, bout, gw, uw, dw):
    c = derive(cfg)
    H, NH, TQ, S, FFI = c["H"], c["NH"], c["TQ"], c["S"], c["FFI"]
    HC, VFW, VFG, FFB = c["HC"], c["VFW"], c["VFG"], c["FFB"]
    NKSLOT, OWN0 = c["NKSLOT"], c["OWN0"]
    f32 = np.float32
    ac = np.ascontiguousarray

    def colslab(W, nb):
        K = W.shape[1]
        return ac(W.reshape(nb, 128, K // 128, 128).transpose(0, 3, 2, 1)
                  .reshape(nb, 128, K)).astype(f32)

    Wq, Wk, Wv = Wqkv[:H], Wqkv[H:2 * H], Wqkv[2 * H:]
    shared = {
        "wq_cs": colslab(Wq, NH),
        "wk_cs": colslab(Wk, NH),
        "wv_rs": ac(Wv.reshape(VFG, VFW, HC, 128).transpose(0, 3, 2, 1)
                    .reshape(VFG, 128, HC * VFW)).astype(f32),
        "wo_cs": colslab(Wout, HC),
        "wg_rows": ac(gw.reshape(FFB, 128, H)).astype(f32),
        "wu_rows": ac(uw.reshape(FFB, 128, H)).astype(f32),
        "wd_rows": ac(dw.reshape(HC, 128, FFI)).astype(f32),
        "bqkv_t": ac(bqkv.reshape(3 * HC, 128).T).astype(f32),
        "bout_t": ac(bout.reshape(HC, 128).T).astype(f32),
        "bv_bc": ac(np.broadcast_to(bqkv[2 * H:3 * H], (128, H))).astype(np.float32).astype(ml_dtypes.bfloat16),
        "tri_in": np.where(np.arange(128)[:, None] <= np.arange(128)[None, :],
                        np.float32(0.0), np.float32(NEG)),
    }
    ind = np.zeros((128, NH * NH), dtype=f32)
    for h in range(NH):
        ind[:, h * NH + h] = 1.0
    shared["ind_b"] = ind.astype(ml_dtypes.bfloat16)
    shared["id_in"] = np.eye(128, dtype=f32)

    NCHUNK = S // TQ
    in_maps = []
    for core in range(2 * NCHUNK):
        b, j = core // NCHUNK, core % NCHUNK
        xb = np.asarray(x[b], dtype=f32)
        plen = j * TQ
        xsT = np.zeros((H, S), dtype=f32)
        if plen:
            xsT[:, :plen] = xb[:plen].T
        xsT[:, S - TQ:] = xb[j * TQ:(j + 1) * TQ].T
        flag = np.zeros((128, NKSLOT), dtype=f32)
        flag[:, plen // 128:OWN0] = NEG
        m = dict(shared)
        m["xs"] = ac(xsT)
        m["flag_in"] = flag
        in_maps.append(m)
    return in_maps


_NC_CACHE = {}


def kernel(L0_attn_x, L0_attn_W_qkv, L0_attn_b_qkv, L0_attn_W_out, L0_attn_b_out,
           L0_ffn_gate_w, L0_ffn_up_w, L0_ffn_down_w):
    cfg = FULL_CFG
    c = derive(cfg)
    x = np.asarray(L0_attn_x, dtype=np.float32)
    in_maps = stage_inputs(cfg, x,
                           np.asarray(L0_attn_W_qkv, dtype=np.float32),
                           np.asarray(L0_attn_b_qkv, dtype=np.float32),
                           np.asarray(L0_attn_W_out, dtype=np.float32),
                           np.asarray(L0_attn_b_out, dtype=np.float32),
                           np.asarray(L0_ffn_gate_w, dtype=np.float32),
                           np.asarray(L0_ffn_up_w, dtype=np.float32),
                           np.asarray(L0_ffn_down_w, dtype=np.float32))
    key = tuple(sorted(cfg.items()))
    if key not in _NC_CACHE:
        _NC_CACHE[key] = build_nc(cfg)
    nc, _ = _NC_CACHE[key]
    res = run_bass_kernel_spmd(nc, in_maps, core_ids=list(range(8)))
    TQ = c["TQ"]
    out = np.empty((2, c["S"], c["H"]), dtype=np.float32)
    for core in range(8):
        b, j = core // 4, core % 4
        out[b, j * TQ:(j + 1) * TQ, :] = np.asarray(res.results[core]["outT"]).T
    return out


# version 18
# speedup vs baseline: 1.0150x; 1.0150x over previous
import numpy as np
import ml_dtypes
from contextlib import ExitStack

import concourse.bass as bass
import concourse.mybir as mybir
from concourse.bass_utils import run_bass_kernel_spmd

F32 = mybir.dt.float32
BF16 = mybir.dt.bfloat16
FP8 = mybir.dt.float8e4
AF = mybir.ActivationFunctionType
ALU = mybir.AluOpType
AX = mybir.AxisListType

CUSH = 70.0
EPS = 1e-5
NEG = -1e30

FULL_CFG = dict(H=2048, NH=16, TQ=1024, S=4096, FFI=8192)


def derive(cfg):
    c = dict(cfg)
    c["HC"] = c["H"] // 128
    c["SCALE"] = float(1.0 / np.sqrt(128.0))
    c["QGW"] = min(512, c["TQ"])
    c["NQG"] = c["TQ"] // c["QGW"]
    c["QSUBG"] = c["QGW"] // 128
    c["MEGA"] = min(512, c["S"])
    c["NMEGA"] = c["S"] // c["MEGA"]
    c["KSUB"] = c["MEGA"] // 128
    c["NKSLOT"] = c["S"] // 128
    c["NQS"] = c["TQ"] // 128
    c["OWN0"] = c["NKSLOT"] - c["NQS"]
    c["VFW"] = min(512, c["H"])
    c["VFG"] = c["H"] // c["VFW"]
    c["FFB"] = c["FFI"] // 128
    c["NCH"] = max(2, c["FFI"] // 2048)
    c["FFHALF"] = c["FFI"] // c["NCH"]
    c["DQ"] = min(2048, c["FFHALF"])
    c["NDQ"] = c["FFHALF"] // c["DQ"]
    return c


class Rec:
    ENGS = ("sync", "gpsimd", "tensor", "vector", "scalar")

    def __init__(self, nc):
        self.nc = nc
        self.stages = [[]]
        self.CN = {}
        self.es = ExitStack()

    def sem(self, name):
        s = self.es.enter_context(self.nc.semaphore(name))
        self.CN[s] = 0
        return s

    def barrier(self):
        if self.stages[-1]:
            self.stages.append([])

    def op(self, eng, fn, waits=(), incs=()):
        rw = tuple((s, int(v)) for (s, v) in waits)
        ri = []
        out = None
        for (s, k) in incs:
            self.CN[s] += k
            ri.append((s, k))
            if out is None:
                out = self.CN[s]
        self.stages[-1].append((eng, fn, rw, tuple(ri)))
        return out

    def build(self):
        for stage in self.stages:
            if not stage:
                continue
            with self.nc.Block() as blk:
                for engname in self.ENGS:
                    ops = [o for o in stage if o[0] == engname]
                    if not ops:
                        continue
                    def body(eng, ops=ops):
                        for (_, fn, waits, incs) in ops:
                            for (s, v) in waits:
                                eng.wait_ge(s, v)
                            inst = fn(eng)
                            for (s, k) in incs:
                                inst.then_inc(s, k)
                    getattr(blk, engname)(body)


def build_nc(cfg):
    c = derive(cfg)
    H, NH, TQ, S, FFI = c["H"], c["NH"], c["TQ"], c["S"], c["FFI"]
    HC, SCALE = c["HC"], c["SCALE"]
    QGW, NQG, QSUBG = c["QGW"], c["NQG"], c["QSUBG"]
    MEGA, NMEGA, KSUB = c["MEGA"], c["NMEGA"], c["KSUB"]
    NKSLOT, OWN0 = c["NKSLOT"], c["OWN0"]
    VFW, VFG = c["VFW"], c["VFG"]
    FFB, FFHALF, DQ, NDQ = c["FFB"], c["FFHALF"], c["DQ"], c["NDQ"]
    NCH = c["NCH"]
    HINV = float(1.0 / H)
    NHB = FFHALF // 128
    SEGW = max(TQ, MEGA)

    PHASES = int(cfg.get("phases", 3))
    FFNLITE = int(cfg.get("ffnlite", 3))
    nc = bass.Bass("TRN2", target_bir_lowering=False)
    R = Rec(nc)

    def din(name, shape, dt=F32):
        return nc.dram_tensor(name, list(shape), dt, kind="ExternalInput").ap()

    xs = din("xs", (H, S))
    wq_cs = din("wq_cs", (NH, 128, H))
    wk_cs = din("wk_cs", (NH, 128, H))
    wv_rs = din("wv_rs", (VFG, 128, HC * VFW))
    wo_cs = din("wo_cs", (HC, 128, H))
    wg_rows = din("wg_rows", (FFB, 128, H))
    wu_rows = din("wu_rows", (FFB, 128, H))
    wd_rows = din("wd_rows", (HC, 128, FFI))
    bqkv_t = din("bqkv_t", (128, 3 * HC))
    bout_t = din("bout_t", (128, HC))
    bv_bc = din("bv_bc", (128, H), BF16)
    tri_c = din("tri_in", (128, 128))
    flag_c = din("flag_in", (128, NKSLOT))
    ind_b = din("ind_b", (128, NH * NH), BF16)
    id_in = din("id_in", (128, 128))

    dacc_hbm = nc.dram_tensor("dacc_hbm", [H, TQ], F32, kind="Internal").ap()
    resid_hbm = nc.dram_tensor("resid_hbm", [H, TQ], F32, kind="Internal").ap()
    outT = nc.dram_tensor("outT", [H, TQ], F32, kind="ExternalOutput").ap()
    DBG = bool(cfg.get("dbg"))
    if DBG:
        dbgK = nc.dram_tensor("dbgK", [NMEGA * 128, NH * MEGA], BF16, kind="ExternalOutput").ap()
        dbgL = nc.dram_tensor("dbgL", [NMEGA * 16, NQG * QGW], F32, kind="ExternalOutput").ap()
        dbgPT = nc.dram_tensor("dbgPT", [NMEGA * 128, NH * KSUB * QGW], BF16, kind="ExternalOutput").ap()

    es = R.es
    sbuf = lambda n, shp, dt: es.enter_context(nc.sbuf_tensor(n, list(shp), dt))
    mkps = lambda n: es.enter_context(nc.psum_tensor(n, [128, 512], F32))

    s_dm = R.sem("s_dm")
    s_dmg = R.sem("s_dmg")
    s_res0 = R.sem("s_res0")
    s_res1 = R.sem("s_res1")
    s_res = [s_res0, s_res1]
    s_out0 = R.sem("s_out0")
    s_out1 = R.sem("s_out1")
    s_out = [s_out0, s_out1]
    s_mm = R.sem("s_mm")
    s_v = R.sem("s_v")
    s_sc = R.sem("s_sc")
    s_rl = R.sem("s_rl")
    s_g = R.sem("s_g")

    CN = R.CN

    class Dstream:
        def __init__(self, name, queue, npar=2):
            self.sems = [R.sem(f"{name}{i}") for i in range(npar)]
            self.queue = queue
            self.n = 0
        def load(self, fn, waits=()):
            s = self.sems[self.n % len(self.sems)]
            self.n += 1
            cnt = R.op(self.queue, fn, waits=waits, incs=((s, 16),))
            return (s, cnt)

    stx = Dstream("stx", "sync", npar=4)
    strow = Dstream("strow", "sync")
    stm = Dstream("stm", "sync")
    swc = Dstream("swc", "gpsimd")
    swv = Dstream("swv", "gpsimd")

    def V(fn, waits=()):
        return R.op("vector", fn, waits=tuple(waits) + ((s_v, CN[s_v]),), incs=((s_v, 1),))

    def SC(fn, waits=()):
        return R.op("scalar", fn, waits=tuple(waits) + ((s_sc, CN[s_sc]),), incs=((s_sc, 1),))

    def G(fn, waits=()):
        return R.op("gpsimd", fn, waits=tuple(waits) + ((s_g, CN[s_g]),), incs=((s_g, 1),))

    def T(fn, waits=(), mark=False):
        return R.op("tensor", fn, waits=waits, incs=((s_mm, 1),) if mark else ())

    Q_sb = sbuf("Q", [128, NH * TQ], BF16)
    bq_sb = sbuf("bq", [128, 3 * HC], F32)
    bo_sb = sbuf("bo", [128, HC], F32)
    tri_sb = sbuf("tri", [128, 128], F32)
    flag_sb = sbuf("flag", [128, NKSLOT], F32)
    indb_sb = sbuf("indb", [128, NH * NH], BF16)
    ones1_sb = sbuf("ones1", [1, 128], F32)
    onesc_sb = sbuf("onesc", [128, 1], F32)
    stat_sb = sbuf("stat", [32, 64], F32)
    rowt_sb = sbuf("rowt", [32, 32], F32)
    biasrow_sb = sbuf("biasrow", [1, 4 * 32], F32)
    bias_sb = sbuf("biasx", [128, KSUB * NH], F32)
    facbc_sb = sbuf("facbc", [128, NH], F32)
    rl0_sb = sbuf("rl0", [1, 2 * 512], F32)
    rL_sb = sbuf("rL", [16, NQG * QGW], F32)
    Lacc_sb = sbuf("Lacc", [16, NQG * QGW], F32)
    id128_sb = sbuf("id128", [128, 128], F32)
    tqT_sb = sbuf("tqT", [128, 512], F32)
    tq8_sb = sbuf("tq8", [128, 512], FP8)
    tqb_sb = sbuf("tqb", [128, 512], BF16)
    bmt_sb = sbuf("bmt", [128, 16], F32)
    rsbt_sb = sbuf("rsbt", [128, 16], F32)
    s2bt_sb = sbuf("s2bt", [128, 16], F32)

    C_QMAX, C_KMAX, C_SB, C_SBN, C_FAC, C_T0, C_T1, C_T2 = range(8)
    stc = lambda col: stat_sb[0:NH, col:col + 1]

    PS_S0 = mkps("ps_s0"); PS_S1 = mkps("ps_s1")
    PS_O0 = mkps("ps_o0"); PS_O1 = mkps("ps_o1")
    PS_L0 = mkps("ps_l0"); PS_L1 = mkps("ps_l1")
    PS_X0 = mkps("ps_x0"); PS_X1 = mkps("ps_x1")
    SBANK = [PS_S0, PS_S1]
    LBANK = [PS_L0, PS_L1]
    XBANK2 = [PS_X0, PS_X1]

    for apv, dst in ((bqkv_t, bq_sb), (bout_t, bo_sb), (tri_c, tri_sb),
                     (flag_c, flag_sb), (id_in, id128_sb)):
        R.op("sync", lambda e, apv=apv, dst=dst: e.dma_start(dst[:, :], apv),
             incs=((s_dm, 16),))
    R.op("gpsimd", lambda e: e.dma_start(indb_sb[:, :], ind_b), incs=((s_dmg, 16),))
    V(lambda e: e.memset(ones1_sb[:, :], 1.0))
    V(lambda e: e.memset(onesc_sb[:, :], 1.0))
    V(lambda e: e.memset(stat_sb[:, :], 0.0))
    V(lambda e: e.memset(rowt_sb[:, :], 0.0))
    V(lambda e: e.memset(biasrow_sb[:, :], 0.0))
    cdm = CN[s_dm]
    cdw = CN[s_dmg]
    R.barrier()
    T(lambda e: e.nop(), waits=((s_dm, cdm), (s_dmg, cdw)))
    V(lambda e: e.nop(), waits=((s_dm, cdm), (s_dmg, cdw)))
    SC(lambda e: e.nop(), waits=((s_dm, cdm), (s_dmg, cdw)))

    def emit_quantT(src_f32_ap, dst_bf16_ap, width=512):
        nj = width // 32
        V(lambda e: e.transpose(tqT_sb[:, 0:width], src_f32_ap))
        V(lambda e: e.tensor_reduce(
            bmt_sb[:, 0:nj], tqT_sb[:, 0:width].rearrange("p (j b) -> p j b", j=nj),
            AX.X, ALU.max, apply_absolute_value=True))
        V(lambda e: e.tensor_scalar(s2bt_sb[:, 0:nj], bmt_sb[:, 0:nj],
                                    1.0 / 448.0, 1e-30, ALU.mult, ALU.max))
        V(lambda e: e.reciprocal(rsbt_sb[:, 0:nj], s2bt_sb[:, 0:nj]))
        V(lambda e: e.tensor_scalar_mul(rsbt_sb[:, 0:nj], rsbt_sb[:, 0:nj], 0.5))
        V(lambda e: e.tensor_scalar_mul(s2bt_sb[:, 0:nj], s2bt_sb[:, 0:nj], 2.0))
        V(lambda e: e.tensor_tensor(
            tq8_sb[:, 0:width], tqT_sb[:, 0:width],
            rsbt_sb[:, 0:nj].broadcast_to([128, nj, 32]), ALU.mult))
        V(lambda e: e.tensor_tensor(
            tqb_sb[:, 0:width], tq8_sb[:, 0:width],
            s2bt_sb[:, 0:nj].broadcast_to([128, nj, 32]), ALU.mult))
        V(lambda e: e.transpose(dst_bf16_ap, tqb_sb[:, 0:width]))

    WQST = {"tmark": [0, 0], "xev": {0: 0, 1: 0}, "gmp": [0, 0]}

    def emit_wquant_v(K, rowbuf, pari, extra_waits=()):
        nkb = K // 32
        V(lambda e: e.tensor_reduce(
            bmw_sb[:, 0:nkb], rowbuf[:, 0:K].rearrange("p (b k) -> p b k", k=32),
            AX.X, ALU.max, apply_absolute_value=True), waits=extra_waits)
        V(lambda e, pari=pari: e.tensor_scalar(s2w_sb[pari][:, 0:nkb], bmw_sb[:, 0:nkb],
                                    1.0 / 448.0, 1e-30, ALU.mult, ALU.max),
          waits=((s_g, WQST["gmp"][pari]),))
        V(lambda e, pari=pari: e.reciprocal(rsw_sb[:, 0:nkb], s2w_sb[pari][:, 0:nkb]))
        V(lambda e, pari=pari: e.tensor_scalar_mul(s2w_sb[pari][:, 0:nkb], s2w_sb[pari][:, 0:nkb], 2.0))
        sm = V(lambda e, pari=pari: e.scalar_tensor_tensor(
            q8w_sb[pari][:, 0:K], rowbuf[:, 0:K], 0.5,
            rsw_sb[:, 0:nkb].broadcast_to([128, nkb, 32]), ALU.mult, ALU.mult))
        gm = G(lambda e, pari=pari: e.tensor_tensor(
            dqr_sb[pari][:, 0:K], q8w_sb[pari][:, 0:K],
            s2w_sb[pari][:, 0:nkb].broadcast_to([128, nkb, 32]), ALU.mult),
            waits=((s_v, sm), (s_mm, WQST["tmark"][pari])))
        WQST["gmp"][pari] = gm
        return (s_g, gm)

    def emit_wquant_t(K, dqbuf, pari, vmw, mmfence=0):
        evm = {}
        ngrp4 = (K + 511) // 512
        for grp in range(ngrp4):
            bank = XBANK2[grp % 2]
            nin = min(4, K // 128 - grp * 4)
            if grp < 2:
                w = [vmw]
                if WQST["xev"][grp % 2]:
                    w.append((s_sc, WQST["xev"][grp % 2]))
            else:
                w = [(s_sc, evm[grp - 2])]
            for i4 in range(nin):
                cc2 = grp * 4 + i4
                tm = T(lambda e, cc2=cc2, i4=i4, bank=bank, pari=pari: e.transpose(
                    bank[:, i4 * 128:(i4 + 1) * 128],
                    dqr_sb[pari][:, cc2 * 128:(cc2 + 1) * 128], id128_sb[:, :]),
                    waits=w if i4 == 0 else (), mark=(i4 == nin - 1))
                w = ()
            evm[grp] = SC(lambda e, grp=grp, nin=nin, bank=bank: e.copy(
                dqbuf[:, grp * 512:grp * 512 + nin * 128], bank[:, 0:nin * 128]),
                waits=((s_mm, tm),) + (((s_mm, mmfence),) if grp == 0 else ()))
        WQST["tmark"][pari] = tm
        for grp in range(ngrp4):
            WQST["xev"][grp % 2] = evm[grp]
        return evm

    def emit_ln(tok0, ntok, xst, xsq, xn_write, xnf_consumer,
                sumbanks, bcbanks, srow_sb, SEG, first_waits=(), stepped=False):
        ngrp = max(1, ntok // 512)
        gw = ntok // ngrp
        NB = len(xst)
        sq_marks = {}
        mm_marks = {}
        dtok = {}
        res = {}
        steps = []

        def chunk_step(cc):
            w = [(s_mm, mm_marks[cc - NB])] if cc >= NB else list(first_waits)
            dtok[cc] = stx.load(lambda e, cc=cc: e.dma_start(
                xst[cc % NB][:, 0:ntok], xs[cc * 128:(cc + 1) * 128, tok0:tok0 + ntok]),
               waits=w)
            sq_marks[cc] = SC(lambda e, cc=cc: e.activation(
                xsq[cc % 2][:, 0:ntok], xst[cc % NB][:, 0:ntok], AF.Square),
                waits=[dtok[cc]] + ([(s_mm, mm_marks[cc - 2])] if cc >= 2 else []))
            for gq in range(ngrp):
                T(lambda e, cc=cc, gq=gq: e.matmul(
                    sumbanks[gq][0][0:1, 0:gw], onesc_sb[:, :],
                    xst[cc % NB][:, gq * gw:(gq + 1) * gw],
                    start=(cc == 0), stop=(cc == HC - 1)),
                  waits=(dtok[cc],) + (tuple(first_waits) if cc == 0 and gq == 0 else ()))
            for gq in range(ngrp):
                mk = T(lambda e, cc=cc, gq=gq: e.matmul(
                    sumbanks[gq][1][0:1, 0:gw], onesc_sb[:, :],
                    xsq[cc % 2][:, gq * gw:(gq + 1) * gw],
                    start=(cc == 0), stop=(cc == HC - 1)),
                  waits=((s_sc, sq_marks[cc]),), mark=(gq == ngrp - 1))
            mm_marks[cc] = CN[s_mm]

        for cc in range(HC):
            steps.append(lambda cc=cc: chunk_step(cc))

        def finish_step():
            smw = [(s_mm, CN[s_mm])]
            for gq in range(ngrp):
                sgq = slice(gq * gw, (gq + 1) * gw)
                V(lambda e, sgq=sgq, gq=gq: e.tensor_scalar_mul(
                    srow_sb[0:1, sgq], sumbanks[gq][0][0:1, 0:gw], HINV), waits=smw)
                smw = ()
                V(lambda e, sgq=sgq, gq=gq: e.tensor_scalar_mul(
                    srow_sb[0:1, 2 * SEG + gq * gw:2 * SEG + (gq + 1) * gw],
                    sumbanks[gq][1][0:1, 0:gw], HINV))
            V(lambda e: e.tensor_mul(srow_sb[0:1, SEG:SEG + ntok],
                                     srow_sb[0:1, 0:ntok], srow_sb[0:1, 0:ntok]))
            V(lambda e: e.tensor_sub(srow_sb[0:1, 2 * SEG:2 * SEG + ntok],
                                     srow_sb[0:1, 2 * SEG:2 * SEG + ntok], srow_sb[0:1, SEG:SEG + ntok]))
            vm = V(lambda e: e.tensor_scalar_add(srow_sb[0:1, 2 * SEG:2 * SEG + ntok], srow_sb[0:1, 2 * SEG:2 * SEG + ntok], EPS))
            SC(lambda e: e.activation(srow_sb[0:1, 2 * SEG:2 * SEG + ntok], srow_sb[0:1, 2 * SEG:2 * SEG + ntok], AF.Ln),
               waits=((s_v, vm),))
            scm = SC(lambda e: e.activation(srow_sb[0:1, 2 * SEG:2 * SEG + ntok], srow_sb[0:1, 2 * SEG:2 * SEG + ntok],
                                            AF.Exp, scale=-0.5))
            vm = V(lambda e: e.scalar_tensor_tensor(
                srow_sb[0:1, 0:ntok], srow_sb[0:1, 0:ntok], -1.0, srow_sb[0:1, 2 * SEG:2 * SEG + ntok],
                ALU.mult, ALU.mult), waits=((s_sc, scm),))
            for gq in range(ngrp):
                br, bs = bcbanks[gq]
                T(lambda e, gq=gq, br=br: e.matmul(
                    br[:, 0:gw], ones1_sb[:, :], srow_sb[0:1, 2 * SEG + gq * gw:2 * SEG + (gq + 1) * gw],
                    start=True, stop=True), waits=((s_v, vm),))
                T(lambda e, gq=gq, bs=bs: e.matmul(
                    bs[:, 0:gw], ones1_sb[:, :], srow_sb[0:1, gq * gw:(gq + 1) * gw],
                    start=True, stop=True), mark=(gq == ngrp - 1))
            res['bcm'] = CN[s_mm]

        v_marks = {}
        c_marks = {}

        def affine_chunk(cc):
            w = ([(s_v, v_marks[cc - NB])] if cc >= NB
                 else [(s_mm, mm_marks[HC - NB + cc]), (s_sc, sq_marks[HC - 1])])
            dtok[cc] = stx.load(lambda e, cc=cc: e.dma_start(
                xst[cc % NB][:, 0:ntok], xs[cc * 128:(cc + 1) * 128, tok0:tok0 + ntok]),
               waits=w)
            first = [dtok[cc], (s_mm, res['bcm'])]
            for gq in range(ngrp):
                br, bs = bcbanks[gq]
                sgq = slice(gq * gw, (gq + 1) * gw)
                V(lambda e, cc=cc, sgq=sgq, br=br: e.tensor_tensor(
                    xst[cc % NB][:, sgq], xst[cc % NB][:, sgq], br[:, 0:gw], ALU.mult),
                  waits=first)
                first = ()
                V(lambda e, cc=cc, sgq=sgq, bs=bs: e.tensor_tensor(
                    xst[cc % NB][:, sgq], xst[cc % NB][:, sgq], bs[:, 0:gw], ALU.add))
            wdst = xn_write(cc)
            if wdst is not None:
                V(lambda e, cc=cc, wdst=wdst: e.tensor_copy(wdst, xst[cc % NB][:, 0:ntok]))
            c_marks[cc] = CN[s_v]
            if xnf_consumer is not None:
                xnf_consumer(cc, xst[cc % NB])
            v_marks[cc] = CN[s_v]
            if cc == HC - 1:
                res['ready'] = CN[s_v]
                res['v'] = v_marks
                res['c'] = c_marks

        steps.append(finish_step)
        for cc in range(HC):
            steps.append(lambda cc=cc: affine_chunk(cc))
        if stepped:
            return steps, res
        for st in steps:
            st()
        return res['ready'], res['v'], res['c']


    def _early_exit(stacks):
        R.op("sync", lambda e: e.nop(),
             waits=((s_res0, CN[s_res0]), (s_res1, CN[s_res1]),
                    (s_dm, CN[s_dm]), (s_dmg, CN[s_dmg])))
        R.build()
        for st in stacks:
            st.close()
        es.close()
        nc.finalize()
        nc._final_counts = {getattr(s, 'name', str(s)): v for s, v in CN.items()}
        return nc, c

    esA = ExitStack()
    asb = lambda n, shp, dt: esA.enter_context(nc.sbuf_tensor(n, list(shp), dt))
    xdq_sb = asb("xdq", [128, HC * TQ], BF16)
    xstA = [asb(f"xstA{i}", [128, TQ], F32) for i in range(2)]
    srowA_sb = asb("srowA", [1, 3 * TQ], F32)

    esA1 = ExitStack()
    xn_sb = esA1.enter_context(nc.sbuf_tensor("xn", [128, HC * TQ], BF16))
    xsqA = [esA1.enter_context(nc.sbuf_tensor(f"xsqA{i}", [128, TQ], F32)) for i in range(2)]
    qsqb_sb = [esA1.enter_context(nc.sbuf_tensor(f"qsqb{i}", [128, TQ], BF16)) for i in range(2)]
    wqb_sb = [esA1.enter_context(nc.sbuf_tensor(f"wqb{i}", [128, H], BF16)) for i in range(2)]

    def a_consume(cc, buf):
        w5 = min(512, TQ)
        for gq in range(TQ // w5):
            emit_quantT(buf[:, gq * w5:(gq + 1) * w5],
                        xdq_sb[:, cc * TQ + gq * w5:cc * TQ + (gq + 1) * w5], width=w5)

    _, xnmarks, xncop = emit_ln(S - TQ, TQ, xstA, xsqA,
            lambda cc: xn_sb[:, cc * TQ:(cc + 1) * TQ],
            a_consume,
            sumbanks=[(PS_X0, PS_X1), (PS_S0, PS_S1)],
            bcbanks=[(PS_O0, PS_O1), (PS_X0, PS_X1)], srow_sb=srowA_sb, SEG=TQ)
    if PHASES == 11:
        return _early_exit([esA1, esA])

    evac_marks = {}
    for h in range(NH):
        w = [(s_v, evac_marks[(h - 2, NQG - 1)])] if h >= 2 else []
        wtok = swc.load(lambda e, h=h: e.dma_start(wqb_sb[h % 2][:, 0:H], wq_cs[h]), waits=w)
        waits = [wtok]
        if h >= 1:
            waits.append((s_v, evac_marks[(h - 1, NQG - 1)]))
        mks = {}
        for k in range(HC):
            for gq in range(NQG):
                bank = SBANK[gq]
                xw2 = list(waits) if (k == 0 and gq == 0) else []
                if h == 0 and gq == 0:
                    xw2.append((s_v, xncop[k]))
                mks[gq] = T(lambda e, h=h, gq=gq, k=k, bank=bank: e.matmul(
                    bank[:, 0:QGW], wqb_sb[h % 2][:, k * 128:(k + 1) * 128],
                    xn_sb[:, k * TQ + gq * QGW:k * TQ + (gq + 1) * QGW],
                    start=(k == 0), stop=(k == HC - 1)),
                  waits=xw2, mark=(k == HC - 1))
        for gq in range(NQG):
            bank = SBANK[gq]
            evac_marks[(h, gq)] = V(lambda e, h=h, gq=gq, bank=bank: e.tensor_scalar_add(
                Q_sb[:, h * TQ + gq * QGW:h * TQ + (gq + 1) * QGW],
                bank[:, 0:QGW], bq_sb[:, h:h + 1]), waits=((s_mm, mks[gq]),))
    R.barrier()

    sqm = {}
    qtm = {}
    for h in range(NH):
        sqm[h] = V(lambda e, h=h: e.tensor_mul(
            qsqb_sb[h % 2][:, 0:TQ], Q_sb[:, h * TQ:(h + 1) * TQ], Q_sb[:, h * TQ:(h + 1) * TQ]),
            waits=[(s_mm, qtm[h - 2])] if h >= 2 else ())
        for gq in range(NQG):
            mk = T(lambda e, h=h, gq=gq: e.matmul(
                [PS_X0, PS_X1][gq][0:NH, 0:QGW], indb_sb[:, h * NH:h * NH + NH],
                qsqb_sb[h % 2][:, gq * QGW:(gq + 1) * QGW],
                start=(h == 0), stop=(h == NH - 1)),
              waits=((s_v, sqm[h]),), mark=True)
        qtm[h] = mk
    V(lambda e: e.reduce_max(stc(C_T0), PS_X0[0:NH, 0:QGW], AX.X), waits=((s_mm, mk),))
    if NQG > 1:
        V(lambda e: e.reduce_max(stc(C_T1), PS_X1[0:NH, 0:QGW], AX.X))
        V(lambda e: e.tensor_max(stc(C_T0), stc(C_T0), stc(C_T1)))
    vm = CN[s_v]
    SC(lambda e: e.activation(stc(C_T0), stc(C_T0), AF.Ln), waits=((s_v, vm),))
    SC(lambda e: e.activation(stc(C_QMAX), stc(C_T0), AF.Exp, scale=0.5))
    R.barrier()
    if PHASES == 12:
        return _early_exit([esA1, esA])
    esA1.close()

    esA2 = ExitStack()
    hh_sb = esA2.enter_context(nc.sbuf_tensor("hh", [128, NHB * TQ], BF16))
    q8w_sb = [esA2.enter_context(nc.sbuf_tensor(f"q8w{i}", [128, 2048], FP8)) for i in range(2)]
    bmw_sb = esA2.enter_context(nc.sbuf_tensor("bmw", [128, 128], F32))
    rsw_sb = esA2.enter_context(nc.sbuf_tensor("rsw", [128, 128], F32))
    s2w_sb = [esA2.enter_context(nc.sbuf_tensor(f"s2w{i}", [128, 128], F32)) for i in range(2)]
    dqr_sb = [esA2.enter_context(nc.sbuf_tensor(f"dqr{i}", [128, 2048], F32)) for i in range(2)]
    wrow_sb = [esA2.enter_context(nc.sbuf_tensor(f"wrow{i}", [128, DQ], F32)) for i in range(2)]
    dqw_sb = [esA2.enter_context(nc.sbuf_tensor(f"dqw{i}", [128, DQ], BF16)) for i in range(2)]
    sg_sb = esA2.enter_context(nc.sbuf_tensor("sg", [128, 2 * QGW], F32))
    hf_sb = esA2.enter_context(nc.sbuf_tensor("hf", [128, QGW], F32))
    dtl_sb = [esA2.enter_context(nc.sbuf_tensor(f"dtl{i}", [128, TQ], F32)) for i in range(2)]

    silu_mark = {}
    hf_mark = {}
    half_fence = {"v": 0}
    prevh = {"duse": None, "dmms": None}
    for half in range(NCH):
        use_marks = {}
        gate_mm = {}
        last_mm = {}
        evall = {}
        hh_ready = {}

        def emit_guw(j, half=half):
            ib = half * NHB + j // 2
            par = j % 2
            evmj = evall[j]
            lastg = -1
            mk = 0
            for k in range(HC):
                for gq in range(NQG):
                    bank = [SBANK, [PS_O0, PS_O1]][par][gq]
                    waits = []
                    if k == 0:
                        if par == 0 and gq in silu_mark:
                            waits.append((s_sc, silu_mark[gq]))
                        if par == 1 and gq in hf_mark:
                            waits.append((s_v, hf_mark[gq]))
                        if j == 0 and gq == 0 and half_fence["v"]:
                            waits.append((s_v, half_fence["v"]))
                    if gq == 0 and k // 4 > lastg:
                        lastg = k // 4
                        waits.append((s_sc, evmj[lastg]))
                    mk = T(lambda e, par=par, gq=gq, k=k, bank=bank: e.matmul(
                        bank[:, 0:QGW], dqw_sb[par][:, k * 128:(k + 1) * 128],
                        xdq_sb[:, k * TQ + gq * QGW:k * TQ + (gq + 1) * QGW],
                        start=(k == 0), stop=(k == HC - 1)),
                      waits=waits, mark=(k == HC - 1))
                    if k == HC - 1:
                        if par == 0:
                            gate_mm[(ib, gq)] = mk
                        last_mm[j] = mk
            if par == 1 and FFNLITE >= 3:
                for gq in range(NQG):
                    gb = SBANK[gq]
                    ub = [PS_O0, PS_O1][gq]
                    sw2 = [(s_mm, gate_mm[(ib, gq)])]
                    if gq in hf_mark:
                        sw2.append((s_v, hf_mark[gq]))
                    scm = SC(lambda e, gb=gb, gq=gq: e.activation(
                        sg_sb[:, gq * QGW:(gq + 1) * QGW], gb[:, 0:QGW], AF.Silu),
                             waits=sw2)
                    silu_mark[gq] = scm
                    hf_mark[gq] = V(lambda e, ub=ub, gq=gq: e.tensor_tensor(
                        hf_sb[:, 0:QGW], sg_sb[:, gq * QGW:(gq + 1) * QGW],
                        ub[:, 0:QGW], ALU.mult),
                      waits=((s_sc, scm), (s_mm, mk)))
                    wdt = min(512, QGW)
                    for sg2 in range(QGW // wdt):
                        loc = (ib % NHB) * TQ + gq * QGW + sg2 * wdt
                        emit_quantT(hf_sb[:, sg2 * wdt:(sg2 + 1) * wdt],
                                    hh_sb[:, loc:loc + wdt], width=wdt)
                hh_ready[ib % NHB] = CN[s_v]

        for j in range(2 * NHB):
            ib = half * NHB + j // 2
            par = j % 2
            if j >= 2:
                w = [(s_v, use_marks[j - 2])]
            elif prevh["duse"]:
                w = [(s_v, prevh["duse"][j])]
            else:
                w = []
            rtok = strow.load(lambda e, ib=ib, par=par, j=j: e.dma_start(
                wrow_sb[par][:, 0:H], (wg_rows if j % 2 == 0 else wu_rows)[ib]), waits=w)
            xw = [rtok]
            mmf = last_mm.get(j - 2, 0)
            if j < 2 and prevh["dmms"]:
                mmf = prevh["dmms"][j]
            if mmf:
                xw.append((s_mm, mmf))
            vmj = emit_wquant_v(H, wrow_sb[par], par, extra_waits=tuple(xw))
            use_marks[j] = CN[s_v]
            if FFNLITE >= 2 and j >= 1:
                emit_guw(j - 1)
            evall[j] = emit_wquant_t(H, dqw_sb[par], par, vmj, mmfence=mmf)
        duse = {}
        dmms = {}
        evmk = {}
        stmk = {}
        mk_gq = {}
        evallD = {}
        def dprequant_v(jj):
            par = jj % 2
            w = [(s_v, use_marks[2 * NHB - 2 + jj])]
            k0 = half * FFHALF
            rtok = strow.load(lambda e, jj=jj, k0=k0, par=par: e.dma_start(
                wrow_sb[par][:, 0:DQ], wd_rows[jj][:, k0:k0 + DQ]), waits=w)
            mmf = last_mm[2 * NHB - 2 + jj]
            vmj = emit_wquant_v(DQ, wrow_sb[par], par,
                                extra_waits=(rtok, (s_mm, mmf)))
            duse[jj] = CN[s_v]
            return vmj, mmf

        vd0 = dprequant_v(0)
        if FFNLITE >= 2:
            emit_guw(2 * NHB - 1)
        guw_sc = CN[s_sc]
        evallD[0] = emit_wquant_t(DQ, dqw_sb[0], 0, vd0[0], mmfence=vd0[1])
        vd1 = dprequant_v(1)
        if PHASES == 13:
            R.barrier()
            return _early_exit([esA2, esA])

        def emit_devac(m, half=half):
            if half > 0:
                dtok1 = stm.load(lambda e, m=m: e.dma_start(
                    dtl_sb[m % 2][:, 0:TQ], dacc_hbm[m * 128:(m + 1) * 128, :]),
                   waits=((s_res0, CN[s_res0]), (s_res1, CN[s_res1])) if m <= 1 else
                         [stmk[m - 2]])
            if half == NCH - 1:
                dtok2 = stm.load(lambda e, m=m: e.dma_start(
                    xstA[m % 2][:, 0:TQ], xs[m * 128:(m + 1) * 128, S - TQ:S]))
            for gq in range(NQG):
                bank = SBANK[(m * NQG + gq) % 2]
                sw = [stmk[m - 2]] if m >= 2 else []
                if half == 0:
                    V(lambda e, m=m, gq=gq, bank=bank: e.tensor_copy(
                        dtl_sb[m % 2][:, gq * QGW:(gq + 1) * QGW], bank[:, 0:QGW]),
                      waits=[(s_mm, mk_gq[gq])] + sw)
                else:
                    V(lambda e, m=m, gq=gq, bank=bank: e.tensor_tensor(
                        dtl_sb[m % 2][:, gq * QGW:(gq + 1) * QGW],
                        dtl_sb[m % 2][:, gq * QGW:(gq + 1) * QGW],
                        bank[:, 0:QGW], ALU.add),
                      waits=[(s_mm, mk_gq[gq]), dtok1] + sw)
                    if half == NCH - 1:
                        V(lambda e, m=m, gq=gq: e.tensor_tensor(
                            dtl_sb[m % 2][:, gq * QGW:(gq + 1) * QGW],
                            dtl_sb[m % 2][:, gq * QGW:(gq + 1) * QGW],
                            xstA[m % 2][:, gq * QGW:(gq + 1) * QGW], ALU.add),
                          waits=(dtok2,))
            evmk[m] = vm2 = CN[s_v]
            dst = dacc_hbm if half < NCH - 1 else resid_hbm
            stmk[m] = (s_res[m % 2], R.op("sync", lambda e, m=m, dst=dst: e.dma_start(
                dst[m * 128:(m + 1) * 128, :], dtl_sb[m % 2][:, 0:TQ]),
                waits=((s_v, vm2),), incs=((s_res[m % 2], 16),)))

        def emit_dmm(jj):
            m, qtr = jj // NDQ, jj % NDQ
            par = jj % 2
            evmj = evallD[jj]
            nk = DQ // 128
            lastg = -1
            for k in range(nk):
                kg = qtr * nk + k
                for gq in range(NQG):
                    bank = SBANK[(m * NQG + gq) % 2]
                    waits = []
                    if k == 0 and gq == 0 and qtr == 0:
                        prev = m - (2 if NQG == 1 else 1)
                        if prev >= 0:
                            waits.append((s_v, evmk[prev]))
                        else:
                            waits.append((s_sc, guw_sc))
                            if half_fence["v"]:
                                waits.append((s_v, half_fence["v"]))
                    if gq == 0:
                        waits.append((s_v, hh_ready[kg]))
                    if gq == 0 and k // 4 > lastg:
                        lastg = k // 4
                        waits.append((s_sc, evmj[lastg]))
                    mk = T(lambda e, par=par, gq=gq, k=k, kg=kg, bank=bank: e.matmul(
                        bank[:, 0:QGW], dqw_sb[par][:, k * 128:(k + 1) * 128],
                        hh_sb[:, kg * TQ + gq * QGW:kg * TQ + (gq + 1) * QGW],
                        start=(kg == 0), stop=(kg == NHB - 1)),
                      waits=waits, mark=(kg == NHB - 1))
                    mk_gq[gq] = mk
            dmms[jj] = CN[s_mm]
            if qtr == NDQ - 1:
                emit_devac(m)

        emit_dmm(0)
        evallD[1] = emit_wquant_t(DQ, dqw_sb[1], 1, vd1[0], mmfence=vd1[1])
        for jj in range(2, HC * NDQ):
            m, qtr = jj // NDQ, jj % NDQ
            par = jj % 2
            w = [(s_v, duse[jj - 2])]
            k0 = half * FFHALF + qtr * DQ
            rtok = strow.load(lambda e, m=m, k0=k0, par=par: e.dma_start(
                wrow_sb[par][:, 0:DQ], wd_rows[m][:, k0:k0 + DQ]), waits=w)
            xw = [rtok]
            mmf = dmms.get(jj - 2, 0)
            if mmf:
                xw.append((s_mm, mmf))
            vmj = emit_wquant_v(DQ, wrow_sb[par], par, extra_waits=tuple(xw))
            duse[jj] = CN[s_v]
            emit_dmm(jj - 1)
            evallD[jj] = emit_wquant_t(DQ, dqw_sb[par], par, vmj, mmfence=mmf)
        emit_dmm(HC * NDQ - 1)
        half_fence["v"] = CN[s_v]
        prevh["duse"] = (duse[HC * NDQ - 2], duse[HC * NDQ - 1])
        prevh["dmms"] = (dmms[HC * NDQ - 2], dmms[HC * NDQ - 1])
        if PHASES == 14:
            R.barrier()
            return _early_exit([esA2, esA])
    esA2.close()
    esA.close()

    esB = ExitStack()
    bsb = lambda n, shp, dt: esB.enter_context(nc.sbuf_tensor(n, list(shp), dt))
    O_sb = bsb("O", [128, NH * TQ], F32)
    bvbc_sb = bsb("bvbc", [128, H], BF16)
    K_sb = bsb("K", [128, NH * MEGA], BF16)
    V_sb = bsb("V", [128, KSUB * H], BF16)
    xnm_sb = bsb("xnm", [128, HC * MEGA], BF16)
    wvb_sb = bsb("wvb", [128, (HC // 2) * VFW], BF16)
    wkb_sb = [bsb(f"wkb{i}", [128, H], BF16) for i in range(2)]
    PT_sb = [bsb(f"PT{i}", [128, QGW], BF16) for i in range(4)]
    ksq_sb = bsb("ksq", [128, MEGA], BF16)
    xstB = [bsb(f"xstB{i}", [128, MEGA], F32) for i in range(4)]
    xsqB = [bsb(f"xsqB{i}", [128, MEGA], F32) for i in range(2)]
    srowB_sb = bsb("srowB", [1, 3 * MEGA], F32)

    if PHASES < 2:
        R.op("sync", lambda e: e.nop(),
             waits=((s_res0, CN[s_res0]), (s_res1, CN[s_res1])))
        R.build()
        esB.close()
        es.close()
        nc.finalize()
        nc._final_counts = {getattr(s, 'name', str(s)): v for s, v in CN.items()}
        return nc, c
    resw = ((s_res0, CN[s_res0]), (s_res1, CN[s_res1]))
    R.op("sync", lambda e: e.nop(), waits=resw)
    R.op("gpsimd", lambda e: e.nop(), waits=resw)
    R.op("gpsimd", lambda e: e.dma_start(bvbc_sb[:, :], bv_bc), incs=((s_dmg, 16),))
    bvtok = (s_dmg, CN[s_dmg])
    T(lambda e: e.nop(), waits=resw)
    SC(lambda e: e.nop(), waits=resw)
    V(lambda e: e.memset(O_sb[:, :], 0.0), waits=resw)
    V(lambda e: e.memset(Lacc_sb[:, :], 0.0))
    R.barrier()

    xnm_ready = {}
    xnm_ready[0], _, _ = emit_ln(0, MEGA, xstB, xsqB,
            lambda cc: xnm_sb[:, cc * MEGA:(cc + 1) * MEGA],
            None,
            sumbanks=[(PS_X0, PS_X1)],
            bcbanks=[(PS_X0, PS_X1)], srow_sb=srowB_sb, SEG=MEGA)
    R.barrier()
    for M in range(NMEGA):
        kevac = {}
        kmmk = {}
        for h in range(NH):
            if h < 2 and M > 0:
                ktok = kpre[h]
            else:
                w = [(s_v, kevac[h - 2])] if h >= 2 else []
                ktok = swc.load(lambda e, h=h: e.dma_start(wkb_sb[h % 2][:, 0:H], wk_cs[h]), waits=w)
            bank = SBANK[h % 2]
            waits = [ktok]
            if h == 0:
                waits.append((s_v, xnm_ready[M]))
            if h >= 2:
                waits.append((s_v, kevac[h - 2]))
            for k in range(HC):
                mk = T(lambda e, h=h, k=k, bank=bank: e.matmul(
                    bank[:, 0:MEGA], wkb_sb[h % 2][:, k * 128:(k + 1) * 128],
                    xnm_sb[:, k * MEGA:(k + 1) * MEGA],
                    start=(k == 0), stop=(k == HC - 1)),
                  waits=waits if k == 0 else (), mark=(k == HC - 1))
            kmmk[h] = mk
            V(lambda e, h=h, bank=bank: e.tensor_scalar_add(
                K_sb[:, h * MEGA:(h + 1) * MEGA], bank[:, 0:MEGA], bq_sb[:, HC + h:HC + h + 1]),
              waits=((s_mm, mk),))
            kevac[h] = V(lambda e, h=h: e.tensor_mul(
                ksq_sb[:, 0:MEGA], K_sb[:, h * MEGA:(h + 1) * MEGA],
                K_sb[:, h * MEGA:(h + 1) * MEGA]))
            mk = T(lambda e, h=h: e.matmul(
                PS_O0[0:NH, 0:MEGA], indb_sb[:, h * NH:h * NH + NH], ksq_sb[:, 0:MEGA],
                start=(h == 0), stop=(h == NH - 1)),
              waits=((s_v, kevac[h]),), mark=(h == NH - 1))
        if DBG:
            R.op("sync", lambda e, M=M: e.dma_start(
                dbgK[M * 128:(M + 1) * 128, :], K_sb[:, :]),
                waits=((s_v, CN[s_v]),), incs=((s_dm, 16),))
        V(lambda e: e.reduce_max(stc(C_T0), PS_O0[0:NH, 0:MEGA], AX.X), waits=((s_mm, mk),))
        vm = CN[s_v]
        knread = vm
        SC(lambda e: e.activation(stc(C_T0), stc(C_T0), AF.Ln), waits=((s_v, vm),))
        scm = SC(lambda e: e.activation(stc(C_KMAX), stc(C_T0), AF.Exp, scale=0.5))
        V(lambda e: e.tensor_mul(stc(C_T1), stc(C_QMAX), stc(C_KMAX)), waits=((s_sc, scm),))
        V(lambda e: e.tensor_scalar_mul(stc(C_T1), stc(C_T1), SCALE))
        if M == 0:
            V(lambda e: e.tensor_copy(stc(C_SB), stc(C_T1)))
        else:
            V(lambda e: e.tensor_max(stc(C_SBN), stc(C_SB), stc(C_T1)))
            vm = V(lambda e: e.tensor_sub(stc(C_T2), stc(C_SB), stc(C_SBN)))
            scm = SC(lambda e: e.activation(stc(C_FAC), stc(C_T2), AF.Exp),
                     waits=((s_v, vm),))
            V(lambda e: e.tensor_copy(stc(C_SB), stc(C_SBN)), waits=((s_sc, scm),))
        V(lambda e: e.transpose(rowt_sb[:, :], stat_sb[0:32, C_SB:C_SB + 32]))
        V(lambda e: e.tensor_scalar(biasrow_sb[0:1, 96:96 + NH], rowt_sb[0:1, 0:NH],
                                    -1.0, CUSH, ALU.mult, ALU.add))
        for ks in range(KSUB):
            V(lambda e, ks=ks, M=M: e.tensor_scalar_add(
                biasrow_sb[0:1, ks * NH:(ks + 1) * NH],
                biasrow_sb[0:1, 96:96 + NH],
                flag_sb[0:1, M * KSUB + ks:M * KSUB + ks + 1]))
        vm = CN[s_v]
        tmB = T(lambda e: e.matmul(PS_S0[:, 0:NH * KSUB], ones1_sb[:, :],
                                   biasrow_sb[0:1, 0:NH * KSUB], start=True, stop=True),
                waits=((s_v, vm),), mark=True)
        V(lambda e: e.tensor_copy(bias_sb[:, 0:NH * KSUB], PS_S0[:, 0:NH * KSUB]),
          waits=((s_mm, tmB),))
        bias_ready = CN[s_v]
        if M > 0:
            vm = V(lambda e: e.transpose(rowt_sb[:, :], stat_sb[0:32, C_FAC:C_FAC + 32]),
                   waits=((s_mm, tmB),))
            tm = T(lambda e: e.matmul(PS_S1[:, 0:NH], ones1_sb[:, :],
                                      rowt_sb[0:1, 0:NH], start=True, stop=True),
                   waits=((s_v, vm),), mark=True)
            V(lambda e: e.tensor_copy(facbc_sb[:, :], PS_S1[:, 0:NH]), waits=((s_mm, tm),))
            for g in range(NQG):
                V(lambda e, g=g: e.tensor_scalar_mul(
                    Lacc_sb[0:NH, g * QGW:(g + 1) * QGW],
                    Lacc_sb[0:NH, g * QGW:(g + 1) * QGW], stc(C_FAC)),
                  waits=((s_dm, CN[s_dm]),))
        vevac = {}
        vln = {}
        nhalf = HC // 2 if HC > 1 else 1
        for F4 in range(VFG):
            for sub in range(2 if HC > 1 else 1):
                idx = F4 * 2 + sub if HC > 1 else F4
                w = [(s_mm, vln[idx - 1])] if idx >= 1 else []
                if idx >= 1:
                    w.append((s_v, vevac[idx - 1]))
                vtok = swv.load(lambda e, F4=F4, c0=sub * nhalf: e.dma_start(
                    wvb_sb[:, 0:nhalf * VFW],
                    wv_rs[F4][:, c0 * VFW:(c0 + nhalf) * VFW]), waits=w)
                c0 = sub * nhalf
                for ks in range(KSUB):
                    bank = [PS_O0, PS_O1, PS_X0, PS_X1][ks]
                    for ci in range(nhalf):
                        k = c0 + ci
                        vw = [vtok] if ci == 0 else []
                        if idx == 0 and ks == 0 and ci == 0:
                            vw.append((s_v, knread))
                        mk = T(lambda e, k=k, ci=ci, ks=ks, bank=bank: e.matmul(
                            bank[:, 0:VFW], xnm_sb[:, k * MEGA + ks * 128:k * MEGA + ks * 128 + 128],
                            wvb_sb[:, ci * VFW:(ci + 1) * VFW],
                            start=(k == 0), stop=(k == HC - 1)),
                          waits=vw,
                          mark=(ci == nhalf - 1))
                    if sub == (1 if HC > 1 else 0):
                        V(lambda e, F4=F4, ks=ks, bank=bank: e.scalar_tensor_tensor(
                            V_sb[:, ks * H + F4 * VFW:ks * H + (F4 + 1) * VFW],
                            bank[:, 0:VFW], 1.0,
                            bvbc_sb[:, F4 * VFW:(F4 + 1) * VFW], ALU.mult, ALU.add),
                          waits=((s_mm, mk), bvtok))
                vevac[idx] = CN[s_v]
                vln[idx] = CN[s_mm]
        vready = CN[s_v]
        lnsteps = []
        if M + 1 < NMEGA:
            lnsteps, lnres = emit_ln(
                (M + 1) * MEGA, MEGA, xstB, xsqB,
                lambda cc: xnm_sb[:, cc * MEGA:(cc + 1) * MEGA],
                None,
                sumbanks=[(PS_X0, PS_X1)],
                bcbanks=[(PS_X0, PS_X1)], srow_sb=srowB_sb, SEG=MEGA,
                first_waits=((s_v, vready),), stepped=True)
            ln_per = -(-len(lnsteps) // max(1, NH - 1))
        exp_marks = {}
        pv_marks = {}
        exp_pt = {}
        oevac = {}
        lmk = {}
        mkq_all = {}

        def emit_qk(h, ks, g):
            if ks == 0 and g == 0 and M > 0:
                for g2 in range(NQG):
                    V(lambda e, h=h, g2=g2: e.tensor_scalar_mul(
                        O_sb[:, h * TQ + g2 * QGW:h * TQ + (g2 + 1) * QGW],
                        O_sb[:, h * TQ + g2 * QGW:h * TQ + (g2 + 1) * QGW],
                        facbc_sb[:, h:h + 1]))
            sidx = (ks * NQG + g) % 2
            sbank = SBANK[sidx]
            waits = [(s_v, vready)]
            if sidx in exp_marks:
                waits.append((s_sc, exp_marks[sidx]))
            mkq_all[(h, ks, g)] = T(lambda e, h=h, g=g, ks=ks, sbank=sbank: e.matmul(
                sbank[:, 0:QGW], K_sb[:, h * MEGA + ks * 128:h * MEGA + ks * 128 + 128],
                Q_sb[:, h * TQ + g * QGW:h * TQ + (g + 1) * QGW],
                start=True, stop=True),
              waits=waits, mark=True)

        def emit_rest(h, ks, g):
            kslot = M * KSUB + ks
            kpar = ks % 2
            j = kslot - OWN0
            mkq = mkq_all[(h, ks, g)]
            bidx = kpar * 2 + g
            sidx = (ks * NQG + g) % 2
            sbank = SBANK[sidx]
            trim = None
            if j >= 0:
                for qq in range(QSUBG):
                    if j == g * QSUBG + qq:
                        trim = V(lambda e, qq=qq, sbank=sbank: e.tensor_tensor(
                            sbank[:, qq * 128:(qq + 1) * 128],
                            sbank[:, qq * 128:(qq + 1) * 128],
                            tri_sb[:, :], ALU.add), waits=((s_mm, mkq),))
            ew = [(s_mm, mkq), (s_v, max(bias_ready, trim or 0))]
            if (kpar, g) in pv_marks:
                ew.append((s_mm, pv_marks[(kpar, g)]))
            em = SC(lambda e, ks=ks, h=h, sbank=sbank, bidx=bidx: e.activation(
                PT_sb[bidx][:, 0:QGW], sbank[:, 0:QGW], AF.Exp,
                bias=bias_sb[:, ks * NH + h:ks * NH + h + 1], scale=SCALE),
                waits=ew)
            exp_marks[sidx] = em
            exp_pt[bidx] = em
            memm = None
            if j >= 0:
                for qq in range(QSUBG):
                    if j > g * QSUBG + qq:
                        memm = V(lambda e, qq=qq, bidx=bidx: e.memset(
                            PT_sb[bidx][:, qq * 128:(qq + 1) * 128], 0.0),
                            waits=((s_sc, em),))
            obank = [PS_O0, PS_O1][g]
            pvw = [(s_sc, em)]
            if memm:
                pvw.append((s_v, memm))
            if ks == 0 and (h - 1, g) in oevac:
                pvw.append((s_v, oevac[(h - 1, g)]))
            if ks == 0 and h == 0:
                pvw.append((s_v, vready))
            T(lambda e, ks=ks, h=h, bidx=bidx, obank=obank: e.matmul(
                obank[:, 0:QGW], V_sb[:, ks * H + h * 128:ks * H + h * 128 + 128],
                PT_sb[bidx][:, 0:QGW], start=(ks == 0), stop=(ks == KSUB - 1)),
              waits=pvw)
            mk2 = T(lambda e, ks=ks, h=h, g=g, bidx=bidx: e.matmul(
                LBANK[g][0:NH, 0:QGW], indb_sb[:, h * NH:h * NH + NH],
                PT_sb[bidx][:, 0:QGW],
                start=(h == 0 and ks == 0),
                stop=(h == NH - 1 and ks == KSUB - 1)),
              mark=True)
            pv_marks[(kpar, g)] = mk2
            lmk[g] = mk2
            if ks == KSUB - 1 and g == NQG - 1:
                for g2 in range(NQG):
                    obank = [PS_O0, PS_O1][g2]
                    oevac[(h, g2)] = V(lambda e, h=h, g2=g2, obank=obank: e.tensor_tensor(
                        O_sb[:, h * TQ + g2 * QGW:h * TQ + (g2 + 1) * QGW],
                        O_sb[:, h * TQ + g2 * QGW:h * TQ + (g2 + 1) * QGW],
                        obank[:, 0:QGW], ALU.add), waits=((s_mm, lmk[g2]),))
                if lnsteps:
                    for st in lnsteps[:ln_per]:
                        st()
                    del lnsteps[:ln_per]

        apos = [(h, ks, g) for h in range(NH) for ks in range(KSUB)
                for g in range(NQG)]
        for bi, pos in enumerate(apos):
            emit_qk(*pos)
            if bi >= 1:
                emit_rest(*apos[bi - 1])
        emit_rest(*apos[-1])
        for g in range(NQG):
            V(lambda e, g=g: e.tensor_tensor(
                Lacc_sb[0:NH, g * QGW:(g + 1) * QGW],
                Lacc_sb[0:NH, g * QGW:(g + 1) * QGW],
                LBANK[g][0:NH, 0:QGW], ALU.add), waits=((s_mm, lmk[g]),))
        if M + 1 < NMEGA:
            assert not lnsteps
            xnm_ready[M + 1] = lnres["ready"]
        if M + 1 < NMEGA:
            kpre = [swc.load(lambda e, hh2=hh2: e.dma_start(
                        wkb_sb[hh2][:, 0:H], wk_cs[hh2]),
                        waits=[(s_mm, kmmk[NH - 2 + hh2])])
                    for hh2 in range(2)]
        if DBG:
            R.op("sync", lambda e, M=M: e.dma_start(
                dbgL[M * 16:M * 16 + 16, :], Lacc_sb[:, :]),
                waits=((s_v, CN[s_v]),), incs=((s_dm, 16),))
        R.barrier()
    esB.close()
    if PHASES < 3:
        R.op("sync", lambda e: e.nop(),
             waits=((s_res0, CN[s_res0]), (s_res1, CN[s_res1]), (s_dm, CN[s_dm]), (s_dmg, CN[s_dmg])))
        R.build()
        es.close()
        nc.finalize()
        nc._final_counts = {getattr(s, 'name', str(s)): v for s, v in CN.items()}
        return nc, c

    esC = ExitStack()
    csb = lambda n, shp, dt: esC.enter_context(nc.sbuf_tensor(n, list(shp), dt))
    Obf_sb = csb("Obf", [128, NH * TQ], BF16)
    wob_sb = [csb(f"wob{i}", [128, H], BF16) for i in range(2)]
    ytl_sb = [csb(f"ytl{i}", [128, QGW], F32) for i in range(2)]
    rtl_sb = [csb(f"rtl{i}", [128, TQ], F32) for i in range(2)]

    for g in range(NQG):
        V(lambda e, g=g: e.reciprocal(rL_sb[0:NH, g * QGW:(g + 1) * QGW],
                                      Lacc_sb[0:NH, g * QGW:(g + 1) * QGW]))
    vm = CN[s_v]
    srl2 = Dstream("srl2", "gpsimd")
    bcmk = {}
    for h in range(NH):
        for g in range(NQG):
            i = h * NQG + g
            slot = i % 2
            bank = [PS_X0, PS_X1][i % 2]
            w = [(s_v, vm)]
            if i >= 2:
                w.append((s_mm, bcmk[i - 2]))
            ltok = srl2.load(lambda e, h=h, g=g, slot=slot: e.dma_start(
                rl0_sb[0:1, slot * QGW:(slot + 1) * QGW],
                rL_sb[h:h + 1, g * QGW:(g + 1) * QGW]), waits=w)
            w2 = [ltok]
            if i >= 2:
                w2.append((s_v, CN[s_v]))
            tm = T(lambda e, slot=slot, bank=bank: e.matmul(
                bank[:, 0:QGW], ones1_sb[:, :],
                rl0_sb[0:1, slot * QGW:(slot + 1) * QGW],
                start=True, stop=True), waits=w2, mark=True)
            bcmk[i] = tm
            V(lambda e, h=h, g=g, bank=bank: e.tensor_tensor(
                Obf_sb[:, h * TQ + g * QGW:h * TQ + (g + 1) * QGW],
                O_sb[:, h * TQ + g * QGW:h * TQ + (g + 1) * QGW],
                bank[:, 0:QGW], ALU.mult), waits=((s_mm, tm),))
    R.barrier()
    yev = {}
    otmk = {}
    for m in range(HC):
        w = [(s_v, yev[(m - 2, NQG - 1)])] if m >= 2 else []
        wtok = swc.load(lambda e, m=m: e.dma_start(wob_sb[m % 2][:, 0:H], wo_cs[m]), waits=w)
        ow = [otmk[m - 2]] if m >= 2 else []
        rtok = stm.load(lambda e, m=m: e.dma_start(rtl_sb[m % 2][:, 0:TQ],
                                                   resid_hbm[m * 128:(m + 1) * 128, :]),
                        waits=[(s_res0, CN[s_res0]), (s_res1, CN[s_res1])] + w + ow)
        waits = [wtok]
        if m >= 1:
            waits.append((s_v, yev[(m - 1, NQG - 1)]))
        mks = {}
        for k in range(HC):
            for gq in range(NQG):
                bank = SBANK[gq]
                mks[gq] = T(lambda e, m=m, gq=gq, k=k, bank=bank: e.matmul(
                    bank[:, 0:QGW], wob_sb[m % 2][:, k * 128:(k + 1) * 128],
                    Obf_sb[:, k * TQ + gq * QGW:k * TQ + (gq + 1) * QGW],
                    start=(k == 0), stop=(k == HC - 1)),
                  waits=waits if (k == 0 and gq == 0) else (), mark=(k == HC - 1))
        for gq in range(NQG):
            bank = SBANK[gq]
            V(lambda e, m=m, gq=gq, bank=bank: e.tensor_scalar_add(
                ytl_sb[gq % 2][:, 0:QGW], bank[:, 0:QGW], bo_sb[:, m:m + 1]),
              waits=((s_mm, mks[gq]), rtok))
            yev[(m, gq)] = V(lambda e, m=m, gq=gq: e.tensor_tensor(
                rtl_sb[m % 2][:, gq * QGW:(gq + 1) * QGW],
                rtl_sb[m % 2][:, gq * QGW:(gq + 1) * QGW],
                ytl_sb[gq % 2][:, 0:QGW], ALU.add))
        vm2 = CN[s_v]
        otmk[m] = (s_out[m % 2], R.op("sync", lambda e, m=m: e.dma_start(
            outT[m * 128:(m + 1) * 128, :], rtl_sb[m % 2][:, 0:TQ]),
            waits=((s_v, vm2),), incs=((s_out[m % 2], 16),)))
    R.op("sync", lambda e: e.nop(),
         waits=((s_out0, CN[s_out0]), (s_out1, CN[s_out1]), (s_dm, CN[s_dm])))

    R.build()
    esC.close()
    es.close()
    nc.finalize()
    nc._final_counts = {getattr(s, 'name', str(s)): v for s, v in CN.items()}
    return nc, c


def stage_inputs(cfg, x, Wqkv, bqkv, Wout, bout, gw, uw, dw):
    c = derive(cfg)
    H, NH, TQ, S, FFI = c["H"], c["NH"], c["TQ"], c["S"], c["FFI"]
    HC, VFW, VFG, FFB = c["HC"], c["VFW"], c["VFG"], c["FFB"]
    NKSLOT, OWN0 = c["NKSLOT"], c["OWN0"]
    f32 = np.float32
    ac = np.ascontiguousarray

    def colslab(W, nb):
        K = W.shape[1]
        return ac(W.reshape(nb, 128, K // 128, 128).transpose(0, 3, 2, 1)
                  .reshape(nb, 128, K)).astype(f32)

    Wq, Wk, Wv = Wqkv[:H], Wqkv[H:2 * H], Wqkv[2 * H:]
    shared = {
        "wq_cs": colslab(Wq, NH),
        "wk_cs": colslab(Wk, NH),
        "wv_rs": ac(Wv.reshape(VFG, VFW, HC, 128).transpose(0, 3, 2, 1)
                    .reshape(VFG, 128, HC * VFW)).astype(f32),
        "wo_cs": colslab(Wout, HC),
        "wg_rows": ac(gw.reshape(FFB, 128, H)).astype(f32),
        "wu_rows": ac(uw.reshape(FFB, 128, H)).astype(f32),
        "wd_rows": ac(dw.reshape(HC, 128, FFI)).astype(f32),
        "bqkv_t": ac(bqkv.reshape(3 * HC, 128).T).astype(f32),
        "bout_t": ac(bout.reshape(HC, 128).T).astype(f32),
        "bv_bc": ac(np.broadcast_to(bqkv[2 * H:3 * H], (128, H))).astype(np.float32).astype(ml_dtypes.bfloat16),
        "tri_in": np.where(np.arange(128)[:, None] <= np.arange(128)[None, :],
                        np.float32(0.0), np.float32(NEG)),
    }
    ind = np.zeros((128, NH * NH), dtype=f32)
    for h in range(NH):
        ind[:, h * NH + h] = 1.0
    shared["ind_b"] = ind.astype(ml_dtypes.bfloat16)
    shared["id_in"] = np.eye(128, dtype=f32)

    NCHUNK = S // TQ
    in_maps = []
    for core in range(2 * NCHUNK):
        b, j = core // NCHUNK, core % NCHUNK
        xb = np.asarray(x[b], dtype=f32)
        plen = j * TQ
        xsT = np.zeros((H, S), dtype=f32)
        if plen:
            xsT[:, :plen] = xb[:plen].T
        xsT[:, S - TQ:] = xb[j * TQ:(j + 1) * TQ].T
        flag = np.zeros((128, NKSLOT), dtype=f32)
        flag[:, plen // 128:OWN0] = NEG
        m = dict(shared)
        m["xs"] = ac(xsT)
        m["flag_in"] = flag
        in_maps.append(m)
    return in_maps


_NC_CACHE = {}


def kernel(L0_attn_x, L0_attn_W_qkv, L0_attn_b_qkv, L0_attn_W_out, L0_attn_b_out,
           L0_ffn_gate_w, L0_ffn_up_w, L0_ffn_down_w):
    cfg = FULL_CFG
    c = derive(cfg)
    x = np.asarray(L0_attn_x, dtype=np.float32)
    in_maps = stage_inputs(cfg, x,
                           np.asarray(L0_attn_W_qkv, dtype=np.float32),
                           np.asarray(L0_attn_b_qkv, dtype=np.float32),
                           np.asarray(L0_attn_W_out, dtype=np.float32),
                           np.asarray(L0_attn_b_out, dtype=np.float32),
                           np.asarray(L0_ffn_gate_w, dtype=np.float32),
                           np.asarray(L0_ffn_up_w, dtype=np.float32),
                           np.asarray(L0_ffn_down_w, dtype=np.float32))
    key = tuple(sorted(cfg.items()))
    if key not in _NC_CACHE:
        _NC_CACHE[key] = build_nc(cfg)
    nc, _ = _NC_CACHE[key]
    res = run_bass_kernel_spmd(nc, in_maps, core_ids=list(range(8)))
    TQ = c["TQ"]
    out = np.empty((2, c["S"], c["H"]), dtype=np.float32)
    for core in range(8):
        b, j = core // 4, core % 4
        out[b, j * TQ:(j + 1) * TQ, :] = np.asarray(res.results[core]["outT"]).T
    return out
